# version 05
# baseline (speedup 1.0000x reference)
import numpy as np
import ml_dtypes
from contextlib import ExitStack

import concourse.bass as bass
import concourse.mybir as mybir
from concourse.bass_utils import run_bass_kernel_spmd

F32 = mybir.dt.float32
BF16 = mybir.dt.bfloat16
AF = mybir.ActivationFunctionType
ALU = mybir.AluOpType
AX = mybir.AxisListType

C = 1024
H = 16
HD = 64
NCC = C // 128
NHP = H // 2
R = 8
SCALING = 16.0 / R
E = 16
M = 512
EPS = 1e-5
EXP_SHIFT = -12.0

ENGS = ("sync", "gpsimd", "tensor", "scalar", "vector")


class Em:
    def __init__(self):
        self.prog = {k: [] for k in ENGS}
        self.cnt = {}
        self.h = {}

    def add_sem(self, name, handle):
        self.h[name] = handle
        self.cnt[name] = 0

    def wait(self, eng, sem, val=None):
        if val is None:
            val = self.cnt[sem]
        assert val <= self.cnt[sem], f"wait on future count {sem} {val} > {self.cnt[sem]}"
        if val <= 0:
            return
        h = self.h[sem]
        self.prog[eng].append(lambda e, h=h, v=val: e.wait_ge(h, v))

    def op(self, eng, build, sem=None, n=1):
        if sem is None or n is None:
            self.prog[eng].append(build)
            return None
        h = self.h[sem]
        self.cnt[sem] += n
        self.prog[eng].append(lambda e, b=build, h=h, n=n: b(e).then_inc(h, n))
        return self.cnt[sem]

    def v(self, sem):
        return self.cnt[sem]

    def vo(self, build):
        self.wait("vector", "sVE")
        return self.op("vector", build, "sVE", 1)


def ts_(i, n=128):
    return slice(i * n, (i + 1) * n)


def build_core(T=4096, n_sub=4, dbg=False):
    NT = T // 128
    NQG = T // 512
    TH = T // 2
    NTH = TH // 128
    NSG = TH // 512

    nc = bass.Bass("TRN2", target_bir_lowering=False, detect_race_conditions=False)

    def din(name, shape, dtype=F32):
        return nc.dram_tensor(name, list(shape), dtype, kind="ExternalInput").ap()

    x_in = din("x", (T, C))
    aWq = din("aWq", (H, C, HD)); aWk = din("aWk", (H, C, HD)); aWv = din("aWv", (H, C, HD))
    aAq = din("aAq", (H, C, R)); aAk = din("aAk", (H, C, R)); aAv = din("aAv", (H, C, R))
    aBq = din("aBq", (H, R, HD)); aBk = din("aBk", (H, R, HD)); aBv = din("aBv", (H, R, HD))
    aWp = din("aWp", (C, C)); abp = din("abp", (C,))
    aAp = din("aAp", (C, R)); aBp = din("aBp", (R, C))
    bWq = din("bWq", (C, C)); bWk = din("bWk", (C, C)); bWv = din("bWv", (C, C)); bWo = din("bWo", (C, C))
    bbq = din("bbq", (C,)); bbk = din("bbk", (C,)); bbv = din("bbv", (C,)); bbo = din("bbo", (C,))
    bthq = din("bthq", (C,)); bthk = din("bthk", (C,))
    gkin = din("gk", (C, E)); gbin = din("gb", (E,))
    wg = din("wg", (E, C, M)); wu = din("wu", (E, C, M)); wd = din("wd", (E, M, C))
    swg = din("swg", (C, 2 * M)); swu = din("swu", (C, 2 * M)); swd = din("swd", (2 * M, C))
    c_ident = din("c_ident", (128, 128), BF16)
    c_ones_bf = din("c_ones_bf", (128, 128), BF16)
    c_ones_f32 = din("c_ones_f32", (1, 128), F32)
    c_masks = din("c_masks", (4, 128, 512), BF16)
    c_ident_f = din("c_ident_f", (128, 128), F32)

    y_out = nc.dram_tensor("y", [T, C], F32, kind="ExternalOutput").ap()
    res_a = nc.dram_tensor("res_a", [T, C], F32, kind="Internal").ap()
    res_b = nc.dram_tensor("res_b", [T, C], F32, kind="Internal").ap()
    kind_dbg = "ExternalOutput" if dbg else "Internal"
    qt_d = nc.dram_tensor("qt_d", [C, T], BF16, kind=kind_dbg).ap()
    kt_d = nc.dram_tensor("kt_d", [C, T], BF16, kind=kind_dbg).ap()
    wq_d = nc.dram_tensor("wq_d", [C, C], BF16, kind="Internal").ap()
    wk_d = nc.dram_tensor("wk_d", [C, C], BF16, kind="Internal").ap()
    wv_d = nc.dram_tensor("wv_d", [C, C], BF16, kind="Internal").ap()
    wp_d = nc.dram_tensor("wp_d", [C, C], BF16, kind="Internal").ap()
    xlt_dbg = nc.dram_tensor("xlt_dbg", [C, T], BF16, kind=kind_dbg).ap() if dbg else None
    ot_dbg = nc.dram_tensor("ot_dbg", [C, T], BF16, kind=kind_dbg).ap() if dbg else None
    iz_dbg = nc.dram_tensor("iz_dbg", [2, 512], F32, kind=kind_dbg).ap() if dbg else None
    p_dbg = nc.dram_tensor("p_dbg", [128, 4096], BF16, kind=kind_dbg).ap() if dbg else None
    n_dbg = nc.dram_tensor("n_dbg", [128, 1024], BF16, kind=kind_dbg).ap() if dbg else None
    bc_dbg = nc.dram_tensor("bc_dbg", [128, 1024], F32, kind=kind_dbg).ap() if dbg else None

    em = Em()

    with ExitStack() as cx:
        e_ = cx.enter_context

        A_cols = max(8 * T, 4 * T + 16384)
        A = e_(nc.sbuf_tensor("arenaA", [128, A_cols], BF16))
        B = e_(nc.sbuf_tensor("arenaB", [128, 8 * T], BF16))
        Ca = e_(nc.sbuf_tensor("arenaC", [128, 24576], BF16))
        W2 = e_(nc.sbuf_tensor("W2", [128, NCC * C], BF16))
        ident = e_(nc.sbuf_tensor("ident", [128, 128], BF16))
        ident_f = e_(nc.sbuf_tensor("ident_f", [128, 128], F32))
        ones_bf = e_(nc.sbuf_tensor("ones_bf", [128, 128], BF16))
        ones_f32 = e_(nc.sbuf_tensor("ones_f32", [1, 128], F32))
        ones_col = e_(nc.sbuf_tensor("ones_col", [128, 1], BF16))
        one_col_f = e_(nc.sbuf_tensor("one_col_f", [128, 1], F32))
        nb_sh = e_(nc.sbuf_tensor("nb_sh", [128, 1], F32))
        eps_t = e_(nc.sbuf_tensor("eps_t", [128, 1], F32))
        lnsc = e_(nc.sbuf_tensor("lnsc", [128, 16], F32))
        qt_stage = e_(nc.sbuf_tensor("qt_stage", [128, 4 * 512], BF16))
        junk_bf = qt_stage[:, 0:C]
        biasrows = e_(nc.sbuf_tensor("biasrows", [128, C], BF16))
        bqv = e_(nc.sbuf_tensor("bqv", [128, 8], F32))
        bkv = e_(nc.sbuf_tensor("bkv", [128, 8], F32))
        thqv = e_(nc.sbuf_tensor("thqv", [128, 8], F32))
        thkv = e_(nc.sbuf_tensor("thkv", [128, 8], F32))
        comb = e_(nc.sbuf_tensor("comb", [128, 16 * E], F32))
        gk_sb = e_(nc.sbuf_tensor("gk_sb", [128, NCC * E], BF16))
        gb_bc = e_(nc.sbuf_tensor("gb_bc", [128, E], F32))
        gb_row = e_(nc.sbuf_tensor("gb_row", [1, E], F32))
        gsc = e_(nc.sbuf_tensor("gsc", [128, 128], F32))

        P0 = e_(nc.psum_tensor("P0", [128, 1024], F32))
        P1 = e_(nc.psum_tensor("P1", [128, 1024], F32))
        P2 = e_(nc.psum_tensor("P2", [128, 1024], F32))
        P3 = e_(nc.psum_tensor("P3", [128, 1024], F32))

        for s in ("sSY", "sGP", "sPE", "sAC", "sVE"):
            em.add_sem(s, e_(nc.semaphore(s)))

        block = e_(nc.Block())

        def A_view(c0, n, dt=BF16):
            ap = A[:, c0:c0 + n]
            return ap.bitcast(dt) if dt is not BF16 else ap

        def B_view(c0, n, dt=BF16):
            ap = B[:, c0:c0 + n]
            return ap.bitcast(dt) if dt is not BF16 else ap

        def C_view(c0, n, dt=BF16):
            ap = Ca[:, c0:c0 + n]
            return ap.bitcast(dt) if dt is not BF16 else ap

        em.op("sync", lambda e: e.dma_start(ident[:, :], c_ident), "sSY", 16)
        em.op("sync", lambda e: e.dma_start(ident_f[:, :], c_ident_f), "sSY", 16)
        em.op("sync", lambda e: e.dma_start(ones_bf[:, :], c_ones_bf), "sSY", 16)
        em.op("sync", lambda e: e.dma_start(ones_f32[:, :], c_ones_f32), "sSY", 16)
        em.op("sync", lambda e: e.dma_start(gb_row[:, :], gbin[None, :]), "sSY", 16)
        for j in range(NCC):
            em.op("sync", lambda e, j=j: e.dma_start(bqv[:, j:j + 1], bbq[ts_(j)][:, None]), "sSY", 16)
            em.op("sync", lambda e, j=j: e.dma_start(bkv[:, j:j + 1], bbk[ts_(j)][:, None]), "sSY", 16)
            em.op("sync", lambda e, j=j: e.dma_start(thqv[:, j:j + 1], bthq[ts_(j)][:, None]), "sSY", 16)
            em.op("sync", lambda e, j=j: e.dma_start(thkv[:, j:j + 1], bthk[ts_(j)][:, None]), "sSY", 16)
        em.op("gpsimd", lambda e: e.dma_start(biasrows[0:1, :], abp[None, :]), "sGP", 16)
        em.op("gpsimd", lambda e: e.dma_start(biasrows[32:33, :], bbv[None, :]), "sGP", 16)
        em.op("gpsimd", lambda e: e.dma_start(biasrows[64:65, :], bbo[None, :]), "sGP", 16)
        em.op("gpsimd", lambda e: e.dma_start(
            gk_sb[:, :].rearrange("p (c e) -> p c e", c=NCC),
            gkin.rearrange("(cc p) e -> p cc e", p=128)), "sGP", 16)
        em.op("vector", lambda e: e.memset(ones_col[:, :], 1.0))
        em.op("vector", lambda e: e.memset(one_col_f[:, :], 1.0))
        em.op("vector", lambda e: e.memset(nb_sh[:, :], EXP_SHIFT))
        em.op("vector", lambda e: e.memset(eps_t[:, :], EPS))
        em.op("vector", lambda e: e.memset(gsc[:, 124:128], 0.0), "sVE", 1)
        em.wait("tensor", "sSY")
        em.op("tensor", lambda e: e.matmul(P3.bitcast(F32)[:, 0:E], ones_f32[:, :], gb_row[:, :],
                                           start=True, stop=True), "sPE", 1)
        em.wait("vector", "sPE")
        em.op("vector", lambda e: e.tensor_copy(gb_bc[:, :], P3.bitcast(F32)[:, 0:E]), "sVE", 1)

        def fold_attn_a():
            a_stage = B_view(0, 2048, F32)
            bpr = Ca[0:8, 0:C]
            aqt = Ca[0:8, C:2 * C]
            ve_guard = 0
            spill_mark = 0
            for (Wt, At, Bt, dstd) in [(aWq, aAq, aBq, wq_d), (aWk, aAk, aBk, wk_d),
                                       (aWv, aAv, aBv, wv_d)]:
                dst = W2[:, :]
                if spill_mark:
                    em.wait("gpsimd", "sSY", spill_mark)
                for cc in range(NCC):
                    em.op("gpsimd", lambda e, Wt=Wt, dst=dst, cc=cc: e.dma_start(
                        dst[:, ts_(cc, C)].rearrange("p (h d) -> p h d", h=H),
                        Wt[:, ts_(cc), :].rearrange("h p d -> p h d")), "sGP", 16)
                if ve_guard:
                    em.wait("sync", "sVE", ve_guard)
                    em.wait("gpsimd", "sVE", ve_guard)
                em.op("sync", lambda e, At=At: e.dma_start(
                    a_stage.rearrange("p (h c r) -> p h c r", h=H, c=NCC),
                    At.rearrange("h (cc p) r -> p h cc r", p=128)), "sSY", 16)
                em.op("gpsimd", lambda e, Bt=Bt: e.dma_start(
                    bpr[:, :].rearrange("r (h d) -> r h d", h=H),
                    Bt.rearrange("h r d -> r h d")), "sGP", 16)
                base_gp = em.v("sGP")
                base_sy = em.v("sSY")
                for h in range(H):
                    em.wait("tensor", "sSY", base_sy)
                    if ve_guard:
                        em.wait("tensor", "sVE", ve_guard)
                    for cc in range(NCC):
                        src = a_stage.rearrange("p (h c r) -> p h c r", h=H, c=NCC)[:, h, cc, :]
                        em.op("tensor", lambda e, src=src, cc=cc: e.matmul(
                            P0.bitcast(F32)[0:8, ts_(cc)], src, ident_f[:, :],
                            start=(cc % 4 == 0), stop=(cc % 4 == 3), is_transpose=True),
                            "sPE", 1 if cc == NCC - 1 else None)
                    pe_tr = em.v("sPE")
                    em.wait("vector", "sPE", pe_tr)
                    em.op("vector", lambda e: e.tensor_scalar_mul(
                        aqt[0:8, :], P0.bitcast(F32)[0:8, :], SCALING), "sVE", 1)
                    ve_aqt = em.v("sVE")
                    em.wait("tensor", "sVE", ve_aqt)
                    em.wait("tensor", "sGP", base_gp)
                    for cc in range(NCC):
                        em.op("tensor", lambda e, cc=cc, h=h: e.matmul(
                            P1.bitcast(F32)[:, cc * HD:(cc + 1) * HD],
                            aqt[0:8, ts_(cc)], bpr[0:8, h * HD:(h + 1) * HD],
                            start=(cc == 0), stop=(cc == NCC - 1)), "sPE", 1 if cc == NCC - 1 else None)
                    pe_mm = em.v("sPE")
                    em.wait("vector", "sPE", pe_mm)
                    dstv = dst.rearrange("p (c hd) -> p c hd", c=NCC)
                    lora = P1.bitcast(F32)[:, 0:NCC * HD].rearrange("p (c hd) -> p c hd", c=NCC)
                    em.op("vector", lambda e, dstv=dstv, lora=lora, h=h: e.tensor_tensor(
                        dstv[:, :, h * HD:(h + 1) * HD], dstv[:, :, h * HD:(h + 1) * HD],
                        lora, op=ALU.add), "sVE", 1)
                    ve_guard = em.v("sVE")
                em.wait("sync", "sVE", ve_guard)
                em.op("sync", lambda e, dstd=dstd: e.dma_start(
                    dstd.rearrange("(cc p) n -> p cc n", p=128),
                    W2[:, :].rearrange("p (c n) -> p c n", c=NCC)), "sSY", 16)
                spill_mark = em.v("sSY")
            em.wait("gpsimd", "sSY", spill_mark)
            for j in range(NCC):
                em.op("gpsimd", lambda e, j=j: e.dma_start(W2[:, ts_(j, C)], aWp[ts_(j), :]), "sGP", 16)
            em.wait("sync", "sVE", ve_guard)
            em.wait("gpsimd", "sVE", ve_guard)
            em.op("sync", lambda e: e.dma_start(
                a_stage.rearrange("p (g r) -> p g r", g=128)[:, 0:NCC, :],
                aAp.rearrange("(cc p) r -> p cc r", p=128)), "sSY", 16)
            em.op("gpsimd", lambda e: e.dma_start(bpr[:, 0:C], aBp), "sGP", 16)
            sy2 = em.v("sSY")
            gp2 = em.v("sGP")
            em.wait("tensor", "sSY", sy2)
            em.wait("tensor", "sVE", ve_guard)
            for cc in range(NCC):
                src = a_stage.rearrange("p (g r) -> p g r", g=128)[:, cc, :]
                em.op("tensor", lambda e, src=src, cc=cc: e.matmul(
                    P0.bitcast(F32)[0:8, ts_(cc)], src, ident_f[:, :],
                    start=(cc % 4 == 0), stop=(cc % 4 == 3), is_transpose=True),
                    "sPE", 1 if cc == NCC - 1 else None)
            em.wait("vector", "sPE")
            em.op("vector", lambda e: e.tensor_scalar_mul(aqt[0:8, :], P0.bitcast(F32)[0:8, :], SCALING), "sVE", 1)
            em.wait("tensor", "sVE")
            em.wait("tensor", "sGP", gp2)
            add_prev = 0
            for cc in range(NCC):
                if add_prev:
                    em.wait("tensor", "sVE", add_prev)
                for nh in range(2):
                    em.op("tensor", lambda e, cc=cc, nh=nh: e.matmul(
                        P1.bitcast(F32)[:, ts_(nh, 512)],
                        aqt[0:8, ts_(cc)], bpr[0:8, ts_(nh, 512)],
                        start=True, stop=True), "sPE", 1 if nh == 1 else None)
                pe_mm = em.v("sPE")
                em.wait("vector", "sPE", pe_mm)
                em.op("vector", lambda e, cc=cc: e.tensor_tensor(
                    W2[:, ts_(cc, C)], W2[:, ts_(cc, C)], P1.bitcast(F32)[:, 0:C],
                    op=ALU.add), "sVE", 1)
                add_prev = em.v("sVE")

            em.wait("sync", "sVE", em.v("sVE"))
            em.op("sync", lambda e: e.dma_start(
                wp_d.rearrange("(cc p) n -> p cc n", p=128),
                W2[:, :].rearrange("p (c n) -> p c n", c=NCC)), "sSY", 16)

        fold_attn_a()

        def emit_ln(src, row0, n_tiles, xlnT, xstg, lnstg):
            em.wait("sync", "sSY")
            em.wait("sync", "sPE")
            pre_pe = em.v("sPE")
            pre_ve = em.v("sVE")
            tr_mark = [0] * n_tiles
            ev_mark = [0] * n_tiles
            xl_mark = [0] * n_tiles
            var_mark = [0] * n_tiles
            for i in range(n_tiles):
                s = i % 2
                if i >= 2:
                    em.wait("sync", "sVE", xl_mark[i - 2])
                em.op("sync", lambda e, i=i, s=s: e.dma_start(
                    xstg[s], src[row0 + i * 128: row0 + (i + 1) * 128, :]), "sSY", 16)
                ld = em.v("sSY")
                em.wait("scalar", "sSY", ld)
                if i >= 2:
                    em.wait("scalar", "sVE", var_mark[i - 2])
                em.op("scalar", lambda e, s=s: e.activation(
                    junk_bf[:, :], xstg[s], AF.Square, accum_out=lnsc[:, 2 + s:3 + s]), "sAC", 1)
                sqm = em.v("sAC")
                em.wait("vector", "sSY", ld)
                em.vo(lambda e, s=s: e.reduce_sum(lnsc[:, 0 + s:1 + s], xstg[s], axis=AX.X))
                em.wait("vector", "sAC", sqm)
                em.vo(lambda e, s=s: e.tensor_scalar_mul(lnsc[:, 4 + s:5 + s], lnsc[:, 0 + s:1 + s], 1.0 / C))
                em.vo(lambda e, s=s: e.tensor_tensor(
                    lnsc[:, 6 + s:7 + s], lnsc[:, 4 + s:5 + s], lnsc[:, 4 + s:5 + s], op=ALU.mult))
                em.vo(lambda e, s=s: e.scalar_tensor_tensor(
                    lnsc[:, 8 + s:9 + s], lnsc[:, 2 + s:3 + s], 1.0 / C, lnsc[:, 6 + s:7 + s],
                    op0=ALU.mult, op1=ALU.subtract))
                vm = em.v("sVE")
                var_mark[i] = vm
                em.wait("scalar", "sVE", vm)
                em.op("scalar", lambda e, s=s: e.activation(
                    lnsc[:, 10 + s:11 + s], lnsc[:, 8 + s:9 + s], AF.Sqrt, bias=eps_t[:, :]), "sAC", 1)
                sq2 = em.v("sAC")
                em.wait("vector", "sAC", sq2)
                em.vo(lambda e, s=s: e.reciprocal(lnsc[:, 12 + s:13 + s], lnsc[:, 10 + s:11 + s]))
                if i >= 2:
                    em.wait("vector", "sPE", tr_mark[i - 2])
                em.vo(lambda e, s=s: e.tensor_scalar(
                    lnstg[s], xstg[s], lnsc[:, 4 + s:5 + s], lnsc[:, 12 + s:13 + s],
                    op0=ALU.subtract, op1=ALU.mult))
                xl_mark[i] = em.v("sVE")
                ptr = P2 if s == 0 else P3
                em.wait("tensor", "sVE", xl_mark[i])
                if i >= 2:
                    em.wait("tensor", "sVE", ev_mark[i - 2])
                for cc in range(NCC):
                    em.op("tensor", lambda e, s=s, cc=cc, ptr=ptr: e.matmul(
                        ptr.bitcast(F32)[:, ts_(cc)], lnstg[s][:, ts_(cc)], ident_f[:, :],
                        start=(cc % 4 == 0), stop=(cc % 4 == 3), is_transpose=True),
                        "sPE", 1 if cc == NCC - 1 else None)
                tr_mark[i] = em.v("sPE")
                em.wait("vector", "sPE", tr_mark[i])
                dstv = xlnT.rearrange("p (c t) -> p c t", c=NCC)[:, :, i * 128:(i + 1) * 128]
                em.op("vector", lambda e, dstv=dstv, ptr=ptr: e.tensor_copy(
                    dstv, ptr.bitcast(F32)[:, :].rearrange("p (c t) -> p c t", c=NCC)), "sVE", 1)
                ev_mark[i] = em.v("sVE")

        def emit_attn(variant, src, dst):
            is_a = variant == "a"
            xlnT = A_view(0, NCC * T)
            xstg = [B_view(k * 2048, 2048, F32) for k in range(2)]
            lnstg = [B_view(4096 + k * 2048, 2048, F32) for k in range(2)]
            emit_ln(src, 0, NT, xlnT, xstg, lnstg)
            xr = xlnT.rearrange("p (c t) -> p c t", c=NCC)
            if dbg == 2 and is_a:
                em.wait("sync", "sVE")
                for ccd in range(NCC):
                    em.op("sync", lambda e, ccd=ccd: e.dma_start(
                        xlt_dbg[ts_(ccd)], xr[:, ccd, :]), "sSY", 16)
                em.wait("tensor", "sSY")

            em.wait("sync", "sPE")
            if is_a:
                for mi in range(4):
                    em.op("sync", lambda e, mi=mi: e.dma_start(
                        C_view(18432 + mi * 512, 512), c_masks[mi]), "sSY", 16)
            wA = W2[:, :]
            wB = C_view(0, NCC * C)
            srcs = {"q": wq_d, "k": wk_d, "v": wv_d, "p": wp_d}
            em.op("sync", lambda e: e.dma_start(
                wA.rearrange("p (c n) -> p c n", c=NCC),
                srcs["q"].rearrange("(cc p) n -> p cc n", p=128)), "sSY", 16)
            em.op("sync", lambda e: e.dma_start(
                wB.rearrange("p (c n) -> p c n", c=NCC),
                srcs["k"].rearrange("(cc p) n -> p cc n", p=128)), "sSY", 16)
            wqk_ld = em.v("sSY")
            mask_load = em.v("sSY")
            qkv_entry_ve = em.v("sVE")
            qkv_entry_ac = em.v("sAC")
            em.wait("tensor", "sVE", qkv_entry_ve)
            em.wait("tensor", "sAC", qkv_entry_ac)
            em.wait("tensor", "sSY", wqk_ld)
            qs_mark = [0, 0, 0, 0]
            ev_rot = [0, 0, 0, 0]
            qk_spill = [[0] * NCC, [0] * NCC]
            qt_pe_done = 0
            rot = 0
            for which, (Wsb, dstd, bvec, thvec) in enumerate(
                    [(wA, qt_d, bqv, thqv), (wB, kt_d, bkv, thkv)]):
                if which == 1:
                    qt_pe_done = em.v("sPE")
                    em.wait("sync", "sPE", qt_pe_done)
                    em.op("sync", lambda e: e.dma_start(
                        wA.rearrange("p (c n) -> p c n", c=NCC),
                        srcs["v"].rearrange("(cc p) n -> p cc n", p=128)), "sSY", 16)
                for j in range(NCC):
                    for qg in range(NQG):
                        pq = (P0 if rot < 2 else P1).bitcast(F32)[:, (rot % 2) * 512:(rot % 2) * 512 + 512]
                        if ev_rot[rot]:
                            em.wait("tensor", "sAC" if is_a else "sVE", ev_rot[rot])
                        for cc in range(NCC):
                            em.op("tensor", lambda e, pq=pq, Wsb=Wsb, cc=cc, j=j, qg=qg: e.matmul(
                                pq, Wsb[:, cc * C + j * 128: cc * C + (j + 1) * 128],
                                xr[:, cc, qg * 512:(qg + 1) * 512],
                                start=(cc == 0), stop=(cc == NCC - 1)),
                                "sPE", 1 if cc == NCC - 1 else None)
                        mm_mark = em.v("sPE")
                        sslot = rot
                        if is_a:
                            em.wait("scalar", "sPE", mm_mark)
                            if qs_mark[sslot]:
                                em.wait("scalar", "sSY", qs_mark[sslot])
                            em.op("scalar", lambda e, pq=pq, sslot=sslot: e.copy(
                                qt_stage[:, ts_(sslot, 512)], pq), "sAC", 1)
                            ev_rot[rot] = em.v("sAC")
                            em.wait("sync", "sAC", ev_rot[rot])
                        else:
                            em.wait("vector", "sPE", mm_mark)
                            if qs_mark[sslot]:
                                em.wait("vector", "sSY", qs_mark[sslot])
                            em.op("vector", lambda e, pq=pq, sslot=sslot, bvec=bvec, thvec=thvec, j=j: e.tensor_scalar(
                                qt_stage[:, ts_(sslot, 512)], pq, bvec[:, j:j + 1], thvec[:, j:j + 1],
                                op0=ALU.add, op1=ALU.is_gt), "sVE", 1)
                            ev_rot[rot] = em.v("sVE")
                            em.wait("sync", "sVE", ev_rot[rot])
                        em.op("sync", lambda e, dstd=dstd, j=j, qg=qg, sslot=sslot: e.dma_start(
                            dstd[ts_(j), qg * 512:(qg + 1) * 512], qt_stage[:, ts_(sslot, 512)]), "sSY", 16)
                        qs_mark[sslot] = em.v("sSY")
                        rot = (rot + 1) % 4
                    qk_spill[which][j] = em.v("sSY")
            wv_ld = em.v("sSY")
            em.wait("tensor", "sSY", wv_ld)
            v_sb = B_view(0, 8 * T)
            vv = v_sb.rearrange("p (t hd) -> p t hd", t=NT)
            vrot_mark = [0, 0]
            for i in range(NT):
                for hh in range(2):
                    pv = (P2 if hh == 0 else P3).bitcast(F32)[:, 0:512]
                    if vrot_mark[hh]:
                        em.wait("tensor", "sAC", vrot_mark[hh])
                    for cc in range(NCC):
                        last = (cc == NCC - 1) and is_a
                        em.op("tensor", lambda e, pv=pv, cc=cc, i=i, hh=hh, last=last: e.matmul(
                            pv, xr[:, cc, i * 128:(i + 1) * 128],
                            wA[:, cc * C + hh * 512: cc * C + hh * 512 + 512],
                            start=(cc == 0), stop=last), "sPE", 1 if last else None)
                    if not is_a:
                        em.op("tensor", lambda e, pv=pv, hh=hh: e.matmul(
                            pv, ones_bf[32:33, :], biasrows[32:33, hh * 512:hh * 512 + 512],
                            start=False, stop=True), "sPE", 1)
                    mmv = em.v("sPE")
                    em.wait("scalar", "sPE", mmv)
                    em.op("scalar", lambda e, pv=pv, i=i, hh=hh: e.copy(
                        v_sb[:, i * C + hh * 512: i * C + hh * 512 + 512], pv), "sAC", 1)
                    vrot_mark[hh] = em.v("sAC")
            v_done = em.v("sAC")
            v_pe_done = em.v("sPE")
            em.wait("sync", "sPE", v_pe_done)
            em.op("sync", lambda e: e.dma_start(
                wA.rearrange("p (c n) -> p c n", c=NCC),
                srcs["p"].rearrange("(cc p) n -> p cc n", p=128)), "sSY", 16)
            wp_ld = em.v("sSY")

            otF = A_view(0, NCC * T)
            otc = otF.rearrange("p (k t) -> p k t", k=NHP)
            qkt = [(C_view(0, T), C_view(T, T)), (C_view(2 * T, T), C_view(3 * T, T))]
            p_sb = [C_view(4 * T + k * 1024, 1024) for k in range(2)]
            mask_t = [C_view(18432 + mi * 512, 512) for mi in range(4)]
            ot_tmp = C_view(20480, 1024)
            invz = C_view(21504, 1024)
            invzf = C_view(22528, 2048, F32)

            em.wait("scalar", "sVE", 5)
            em.wait("vector", "sSY", mask_load)
            em.wait("tensor", "sVE", em.v("sVE"))
            attn_entry_pe = em.v("sPE")
            slot_free = [attn_entry_pe, attn_entry_pe]
            slot_loaded = [0, 0]
            norm_ve_prev = 0
            evac_prev = 0

            for p in range(NHP):
                sl = p % 2
                em.wait("gpsimd", "sPE", slot_free[sl])
                em.wait("gpsimd", "sSY", qk_spill[0][p])
                em.wait("gpsimd", "sSY", qk_spill[1][p])
                em.op("gpsimd", lambda e, p=p, sl=sl: e.dma_start(qkt[sl][0][:, :], qt_d[ts_(p), :]), "sGP", 16)
                em.op("gpsimd", lambda e, p=p, sl=sl: e.dma_start(qkt[sl][1][:, :], kt_d[ts_(p), :]), "sGP", 16)
                slot_loaded[sl] = em.v("sGP")
                em.wait("tensor", "sGP", slot_loaded[sl])
                if p == 0:
                    em.wait("tensor", "sAC", v_done)

                qg_seq = ([0] + list(range(NQG))) if p == 0 else list(range(NQG))
                for qg in qg_seq:
                    kcmax = min(4 * (qg + 1), NT) if is_a else NT
                    sac_base = em.v("sAC")
                    exp_marks = {}
                    pv_started = False

                    def emit_S(kc, qg=qg, sl=sl):
                        sps = (P0 if kc % 2 == 0 else P1).bitcast(F32)
                        if kc < 2:
                            em.wait("tensor", "sAC", sac_base)
                        else:
                            em.wait("tensor", "sAC", exp_marks[kc - 2])
                        masked = is_a and kc >= 4 * qg
                        em.op("tensor", lambda e, sps=sps, kc=kc, masked=masked: e.matmul(
                            sps[:, 0:512], qkt[sl][1][0:64, ts_(kc)], qkt[sl][0][0:64, ts_(qg, 512)],
                            start=True, stop=not masked, tile_position=(0, 0)), "sPE", None)
                        em.op("tensor", lambda e, sps=sps, kc=kc, masked=masked: e.matmul(
                            sps[:, 512:1024], qkt[sl][1][64:128, ts_(kc)], qkt[sl][0][64:128, ts_(qg, 512)],
                            start=True, stop=not masked, tile_position=(64, 0)), "sPE", 1 if not masked else None)
                        if masked:
                            mi = kc - 4 * qg
                            em.op("tensor", lambda e, sps=sps, mi=mi: e.matmul(
                                sps[:, 0:512], ident[:, :], mask_t[mi],
                                start=False, stop=True), "sPE", None)
                            em.op("tensor", lambda e, sps=sps, mi=mi: e.matmul(
                                sps[:, 512:1024], ident[:, :], mask_t[mi],
                                start=False, stop=True), "sPE", 1)
                        s_mark = em.v("sPE")
                        em.wait("scalar", "sPE", s_mark)
                        if is_a:
                            em.op("scalar", lambda e, sps=sps, kc=kc: e.activation(
                                p_sb[kc % 2][:, :], sps[:, :], AF.Exp, bias=nb_sh[:, :], scale=0.125),
                                "sAC", 1)
                        else:
                            em.op("scalar", lambda e, sps=sps, kc=kc: e.activation(
                                p_sb[kc % 2][:, :], sps[:, :], AF.Sigmoid, scale=0.125), "sAC", 1)
                        exp_marks[kc] = em.v("sAC")

                    def emit_PV(kc, qg=qg, p=p):
                        nonlocal pv_started, evac_prev
                        em.wait("tensor", "sAC", exp_marks[kc])
                        if not pv_started:
                            if evac_prev:
                                em.wait("tensor", "sAC" if is_a else "sVE", evac_prev)
                            if is_a and norm_ve_prev:
                                em.wait("tensor", "sVE", norm_ve_prev)
                        st = not pv_started
                        sp = (kc == kcmax - 1)
                        pv_started = True
                        em.op("tensor", lambda e, kc=kc, st=st, sp=sp, p=p: e.matmul(
                            P2.bitcast(F32)[0:64, 0:512],
                            vv[:, kc, (2 * p) * HD:(2 * p) * HD + HD],
                            p_sb[kc % 2][:, 0:512],
                            start=st, stop=sp, tile_position=(0, 0)), "sPE", None)
                        em.op("tensor", lambda e, kc=kc, st=st, sp=sp, p=p: e.matmul(
                            P2.bitcast(F32)[64:128, 512:1024],
                            vv[:, kc, (2 * p + 1) * HD:(2 * p + 1) * HD + HD],
                            p_sb[kc % 2][:, 512:1024],
                            start=st, stop=sp, tile_position=(0, 64)),
                            "sPE", None if is_a else (1 if sp else None))
                        if is_a:
                            em.op("tensor", lambda e, kc=kc, st=st, sp=sp: e.matmul(
                                P3.bitcast(F32)[0:1, 0:512], ones_col[:, :], p_sb[kc % 2][:, 0:512],
                                start=st, stop=sp, tile_position=(0, 0)), "sPE", None)
                            em.op("tensor", lambda e, kc=kc, st=st, sp=sp: e.matmul(
                                P3.bitcast(F32)[32:33, 512:1024], ones_col[:, :], p_sb[kc % 2][:, 512:1024],
                                start=st, stop=sp, tile_position=(0, 32)),
                                "sPE", 1 if sp else None)

                    for kc in range(kcmax):
                        emit_S(kc)
                        if kc >= 1:
                            emit_PV(kc - 1)
                    emit_PV(kcmax - 1)
                    pv_mark = em.v("sPE")
                    if is_a:
                        em.wait("scalar", "sPE", pv_mark)
                        em.op("scalar", lambda e: e.activation(
                            invzf[0:1, 0:512], P3.bitcast(F32)[0:1, 0:512], AF.Ln))
                        em.op("scalar", lambda e: e.activation(
                            invzf[32:33, 512:1024], P3.bitcast(F32)[32:33, 512:1024], AF.Ln), "sAC", 1)
                        em.wait("scalar", "sAC")
                        em.op("scalar", lambda e: e.activation(
                            invz[0:1, 0:512], invzf[0:1, 0:512], AF.Exp, scale=-1.0))
                        em.op("scalar", lambda e: e.activation(
                            invz[32:33, 512:1024], invzf[32:33, 512:1024], AF.Exp, scale=-1.0), "sAC", 1)
                        iv_mark = em.v("sAC")
                        if dbg == 6 and is_a and p == 0 and qg == 0:
                            em.wait("sync", "sVE", iv_mark)
                            em.op("sync", lambda e: e.dma_start(iz_dbg.rearrange("a n -> (a n)")[None, :], invzf[0:1, :]), "sSY", 16)
                            em.op("sync", lambda e: e.dma_start(p_dbg[:, 0:1024], p_sb[0][:, :]), "sSY", 16)
                            em.op("sync", lambda e: e.dma_start(p_dbg[:, 1024:2048], p_sb[1][:, :]), "sSY", 16)
                            em.wait("scalar", "sSY")
                            em.wait("vector", "sSY")
                            em.wait("tensor", "sSY")
                        em.wait("scalar", "sPE", pv_mark)
                        em.op("scalar", lambda e: e.copy(ot_tmp[0:64, 0:512], P2.bitcast(F32)[0:64, 0:512]))
                        em.op("scalar", lambda e: e.copy(ot_tmp[64:128, 0:512], P2.bitcast(F32)[64:128, 512:1024]),
                              "sAC", 1)
                        evac_prev = em.v("sAC")
                        em.wait("tensor", "sAC", iv_mark)
                        em.wait("tensor", "sAC", evac_prev)
                        em.op("tensor", lambda e: e.matmul(
                            P2.bitcast(F32)[0:64, 0:512], ones_bf[0:1, 0:64], invz[0:1, 0:512],
                            start=True, stop=True, tile_position=(0, 0)), "sPE", None)
                        em.op("tensor", lambda e: e.matmul(
                            P2.bitcast(F32)[64:128, 0:512], ones_bf[32:33, 0:64], invz[32:33, 512:1024],
                            start=True, stop=True, tile_position=(32, 64)), "sPE", 1)
                        bc_mark = em.v("sPE")
                        em.wait("vector", "sAC", evac_prev)
                        em.wait("vector", "sPE", bc_mark)
                        em.op("vector", lambda e, p=p, qg=qg: e.tensor_tensor(
                            otc[:, p, ts_(qg, 512)], ot_tmp[:, 0:512],
                            P2.bitcast(F32)[:, 0:512], op=ALU.mult), "sVE", 1)
                        norm_ve_prev = em.v("sVE")
                        if dbg == 5 and is_a and p == 0 and qg == 0:
                            em.op("vector", lambda e: e.tensor_copy(bc_dbg and None or None, None) if False else e.tensor_copy(
                                qt_stage[:, 0:1024].bitcast(F32)[:, 0:256], P0.bitcast(F32)[:, 0:256]), None)
                            em.prog["vector"].pop()
                            em.wait("sync", "sVE", norm_ve_prev)
                            em.op("sync", lambda e: e.dma_start(n_dbg[:, 0:512], otc[:, 0, 0:512]), "sSY", 16)
                            em.op("sync", lambda e: e.dma_start(n_dbg[:, 512:1024], ot_tmp[:, 0:512]), "sSY", 16)
                            em.wait("vector", "sVE", norm_ve_prev)
                            em.op("vector", lambda e: e.tensor_copy(invzf[:, 0:1024], P0.bitcast(F32)[:, 0:1024]), "sVE", 1)
                            em.wait("sync", "sVE")
                            em.op("sync", lambda e: e.dma_start(bc_dbg, invzf[:, 0:1024]), "sSY", 16)
                            em.wait("tensor", "sSY")
                            em.wait("vector", "sSY")
                            em.wait("scalar", "sSY")
                    else:
                        em.wait("vector", "sPE", pv_mark)
                        em.op("vector", lambda e, p=p, qg=qg: e.tensor_copy(
                            otc[0:64, p, ts_(qg, 512)], P2.bitcast(F32)[0:64, 0:512]))
                        em.op("vector", lambda e, p=p, qg=qg: e.tensor_copy(
                            otc[64:128, p, ts_(qg, 512)], P2.bitcast(F32)[64:128, 512:1024]), "sVE", 1)
                        evac_prev = em.v("sVE")
                slot_free[sl] = em.v("sPE")

            attn_ve = em.v("sVE")
            attn_ac = em.v("sAC")
            attn_pe = em.v("sPE")

            if dbg and is_a:
                em.wait("sync", "sVE", attn_ve)
                em.wait("sync", "sAC", attn_ac)
                for ccd in range(NCC):
                    em.op("sync", lambda e, ccd=ccd: e.dma_start(
                        ot_dbg[ts_(ccd)], otc[:, ccd, :]), "sSY", 16)
            xn = [B_view(k * 2048, 2048, F32) for k in range(2)]
            em.wait("sync", "sPE", attn_pe)
            em.wait("tensor", "sSY", wp_ld)
            pj_ev = [0, 0]
            st_mark = [0, 0]
            brow = 0 if is_a else 64
            first = True
            for i in range(NT):
                s = i % 2
                if st_mark[s]:
                    em.wait("sync", "sSY", st_mark[s])
                em.op("sync", lambda e, i=i, s=s: e.dma_start(xn[s], src[ts_(i), :]), "sSY", 16)
                ld = em.v("sSY")
                for ch in range(2):
                    pp = P1.bitcast(F32)[:, ch * 512:ch * 512 + 512]
                    if first:
                        em.wait("tensor", "sVE", attn_ve)
                        em.wait("tensor", "sAC", attn_ac)
                        first = False
                    if pj_ev[ch]:
                        em.wait("tensor", "sVE", pj_ev[ch])
                    for j in range(NCC):
                        em.op("tensor", lambda e, pp=pp, j=j, i=i, ch=ch: e.matmul(
                            pp, otc[:, j, ts_(i)], wA[:, j * C + ch * 512: j * C + ch * 512 + 512],
                            start=(j == 0), stop=False), "sPE", None)
                    em.op("tensor", lambda e, pp=pp, ch=ch, brow=brow: e.matmul(
                        pp, ones_bf[brow:brow + 1, :], biasrows[brow:brow + 1, ch * 512:ch * 512 + 512],
                        start=False, stop=True), "sPE", 1)
                    mmk = em.v("sPE")
                    em.wait("vector", "sPE", mmk)
                    em.wait("vector", "sSY", ld)
                    em.op("vector", lambda e, pp=pp, s=s, ch=ch: e.scalar_tensor_tensor(
                        xn[s][:, ch * 512:ch * 512 + 512], pp, 0.0,
                        xn[s][:, ch * 512:ch * 512 + 512], op0=ALU.add, op1=ALU.add), "sVE", 1)
                    pj_ev[ch] = em.v("sVE")
                em.wait("sync", "sVE", pj_ev[1])
                em.op("sync", lambda e, i=i, s=s: e.dma_start(dst[ts_(i), :], xn[s]), "sSY", 16)
                st_mark[s] = em.v("sSY")

        def emit_ffn(src, dst):
            for half in range(2):
                row0 = half * TH
                xlnTh = A_view(0, NCC * TH)
                base = NCC * TH
                xstg = [A_view(base + k * 2048, 2048, F32) for k in range(2)]
                lnstg = [A_view(base + 4096 + k * 2048, 2048, F32) for k in range(2)]
                h_sb = A_view(base + 8192, 2048)
                s_sb = [A_view(base + 10240 + k * 512, 512) for k in range(2)]
                y_full = B_view(0, 8 * T, F32)
                y_ap = y_full.rearrange("p (t c) -> p t c", c=C)[:, 0:NTH, :]

                emit_ln(src, row0, NTH, xlnTh, xstg, lnstg)
                xr = xlnTh.rearrange("p (c t) -> p c t", c=NCC)

                em.wait("tensor", "sVE")
                em.wait("tensor", "sAC")
                gate_tile_mark = {}
                sg_prev = 0
                chain_prev = 0
                for i in range(NTH):
                    pl = P3.bitcast(F32)[:, 0:E]
                    if sg_prev:
                        em.wait("tensor", "sAC", sg_prev)
                    for cc in range(NCC):
                        em.op("tensor", lambda e, pl=pl, cc=cc, i=i: e.matmul(
                            pl, xr[:, cc, ts_(i)], gk_sb[:, cc * E:(cc + 1) * E],
                            start=(cc == 0), stop=(cc == NCC - 1)), "sPE", 1 if cc == NCC - 1 else None)
                    lg = em.v("sPE")
                    em.wait("scalar", "sPE", lg)
                    if chain_prev:
                        em.wait("scalar", "sVE", chain_prev)
                    em.op("scalar", lambda e, pl=pl: e.activation(gsc[:, 0:E], pl, AF.Sigmoid), "sAC", 1)
                    sg_prev = em.v("sAC")
                    g = gsc
                    s4 = g[:, 0:E].rearrange("p (gr f) -> p gr f", gr=4)
                    em.wait("vector", "sAC", sg_prev)
                    em.vo(lambda e: e.tensor_tensor(g[:, 0:E], g[:, 0:E], gb_bc[:, :], op=ALU.add))
                    em.vo(lambda e: e.tensor_tensor(g[:, 20:24], s4[:, :, 0], s4[:, :, 1], op=ALU.max))
                    em.vo(lambda e: e.tensor_tensor(g[:, 24:28], s4[:, :, 0], s4[:, :, 1], op=ALU.min))
                    em.vo(lambda e: e.tensor_tensor(g[:, 28:32], s4[:, :, 2], s4[:, :, 3], op=ALU.max))
                    em.vo(lambda e: e.tensor_tensor(g[:, 32:36], s4[:, :, 2], s4[:, :, 3], op=ALU.min))
                    em.vo(lambda e: e.tensor_tensor(g[:, 36:40], g[:, 20:24], g[:, 28:32], op=ALU.max))
                    em.vo(lambda e: e.tensor_tensor(g[:, 40:44], g[:, 20:24], g[:, 28:32], op=ALU.min))
                    em.vo(lambda e: e.tensor_tensor(g[:, 44:48], g[:, 24:28], g[:, 32:36], op=ALU.max))
                    em.vo(lambda e: e.tensor_tensor(g[:, 44:48], g[:, 44:48], g[:, 40:44], op=ALU.max))
                    em.vo(lambda e: e.tensor_tensor(g[:, 16:20], g[:, 36:40], g[:, 44:48], op=ALU.add))
                    t2 = g[:, 16:20]
                    em.vo(lambda e: e.tensor_tensor(g[:, 49:50], t2[:, 0:1], t2[:, 1:2], op=ALU.max))
                    em.vo(lambda e: e.tensor_tensor(g[:, 50:51], t2[:, 0:1], t2[:, 1:2], op=ALU.min))
                    em.vo(lambda e: e.tensor_tensor(g[:, 51:52], t2[:, 2:3], t2[:, 3:4], op=ALU.max))
                    em.vo(lambda e: e.tensor_tensor(g[:, 52:53], t2[:, 2:3], t2[:, 3:4], op=ALU.min))
                    em.vo(lambda e: e.tensor_tensor(g[:, 53:54], g[:, 49:50], g[:, 51:52], op=ALU.min))
                    em.vo(lambda e: e.tensor_tensor(g[:, 48:49], g[:, 50:51], g[:, 52:53], op=ALU.max))
                    em.vo(lambda e: e.tensor_tensor(g[:, 48:49], g[:, 48:49], g[:, 53:54], op=ALU.max))
                    em.vo(lambda e: e.tensor_single_scalar(g[:, 54:58], t2, g[:, 48:49], op=ALU.is_ge))
                    for jj in range(4):
                        em.vo(lambda e, jj=jj: e.tensor_copy(
                            g[:, 64:80].rearrange("p (a b) -> p a b", a=4)[:, :, jj], g[:, 54:58]))
                    em.vo(lambda e: e.tensor_tensor(g[:, 80:96], g[:, 0:E], g[:, 64:80], op=ALU.mult))
                    em.vo(lambda e: e.reduce_max(g[:, 58:59], g[:, 80:96], axis=AX.X))
                    em.vo(lambda e: e.tensor_single_scalar(g[:, 112:128], g[:, 80:96], g[:, 58:59], op=ALU.is_ge))
                    em.vo(lambda e: e.scalar_tensor_tensor(
                        g[:, 96:112], g[:, 112:128], -1e30, g[:, 80:96], op0=ALU.mult, op1=ALU.add))
                    em.vo(lambda e: e.reduce_max(g[:, 58:59], g[:, 96:112], axis=AX.X))
                    for _ in range(2):
                        em.vo(lambda e: e.tensor_single_scalar(g[:, 112:128], g[:, 96:112], g[:, 58:59], op=ALU.is_ge))
                        em.vo(lambda e: e.scalar_tensor_tensor(
                            g[:, 96:112], g[:, 112:128], -1e30, g[:, 96:112], op0=ALU.mult, op1=ALU.add))
                        em.vo(lambda e: e.reduce_max(g[:, 58:59], g[:, 96:112], axis=AX.X))
                    em.vo(lambda e: e.tensor_single_scalar(
                        g[:, 96:112], g[:, 80:96], g[:, 58:59], op=ALU.is_ge))
                    em.vo(lambda e: e.tensor_tensor(g[:, 96:112], g[:, 96:112], g[:, 80:96], op=ALU.mult))
                    em.vo(lambda e: e.reduce_sum(g[:, 60:61], g[:, 96:112], axis=AX.X))
                    em.vo(lambda e: e.tensor_scalar_add(g[:, 60:61], g[:, 60:61], 1e-20))
                    em.vo(lambda e: e.reciprocal(g[:, 61:62], g[:, 60:61]))
                    em.vo(lambda e: e.tensor_scalar_mul(g[:, 61:62], g[:, 61:62], 2.5))
                    em.vo(lambda e, i=i: e.tensor_single_scalar(
                        comb[:, ts_(i, E)], g[:, 96:112], g[:, 61:62], op=ALU.mult))
                    chain_prev = em.v("sVE")
                    gate_tile_mark[i] = chain_prev
                gate_done = em.v("sVE")

                em.op("sync", lambda e, row0=row0: e.dma_start(
                    y_ap, src[row0:row0 + TH, :].rearrange("(t p) c -> p t c", p=128)), "sSY", 16)
                y_ld = em.v("sSY")
                em.wait("vector", "sSY", y_ld)

                wgb = [C_view(k * 12288, 4096) for k in range(2)]
                wub = [C_view(k * 12288 + 4096, 4096) for k in range(2)]
                wdb = [C_view(k * 12288 + 8192, 4096) for k in range(2)]
                wslot_free = [em.v("sPE"), em.v("sPE")]
                drot_mark = [0, 0, 0, 0]
                y_sl_mark = {}
                silu_mark = [0, 0]
                hmul_mark = [0, 0]

                def wsrcs(e_idx):
                    if e_idx < E:
                        return (wg[e_idx].rearrange("(cc p) m -> p cc m", p=128),
                                wu[e_idx].rearrange("(cc p) m -> p cc m", p=128),
                                wd[e_idx].rearrange("(mc p) c -> p mc c", p=128))
                    hh2 = e_idx - E
                    return (swg[:, hh2 * M:(hh2 + 1) * M].rearrange("(cc p) m -> p cc m", p=128),
                            swu[:, hh2 * M:(hh2 + 1) * M].rearrange("(cc p) m -> p cc m", p=128),
                            swd[hh2 * M:(hh2 + 1) * M, :].rearrange("(mc p) c -> p mc c", p=128))

                for ei in range(E + 2):
                    ws = ei % 2
                    em.wait("gpsimd", "sPE", wslot_free[ws])
                    sg_, su_, sd_ = wsrcs(ei)
                    em.op("gpsimd", lambda e, ws=ws, sg_=sg_: e.dma_start(
                        wgb[ws].rearrange("p (c m) -> p c m", c=NCC), sg_), "sGP", 16)
                    em.op("gpsimd", lambda e, ws=ws, su_=su_: e.dma_start(
                        wub[ws].rearrange("p (c m) -> p c m", c=NCC), su_), "sGP", 16)
                    em.op("gpsimd", lambda e, ws=ws, sd_=sd_: e.dma_start(
                        wdb[ws].rearrange("p (m c) -> p m c", m=4), sd_), "sGP", 16)
                    wload = em.v("sGP")
                    ce_col = one_col_f[:, :] if ei >= E else None
                    em.wait("tensor", "sGP", wload)
                    for sg in range(NSG):
                        if ei == 0:
                            em.wait("vector", "sVE", gate_tile_mark[sg * 4 + 3])
                        for m4 in range(4):
                            sl2 = m4 % 2
                            pg = P0.bitcast(F32)[:, sl2 * 512:sl2 * 512 + 512]
                            pu = P1.bitcast(F32)[:, sl2 * 512:sl2 * 512 + 512]
                            if silu_mark[sl2]:
                                em.wait("tensor", "sAC", silu_mark[sl2])
                            if hmul_mark[sl2]:
                                em.wait("tensor", "sVE", hmul_mark[sl2])
                            for cc in range(NCC):
                                em.op("tensor", lambda e, pg=pg, ws=ws, cc=cc, m4=m4, sg=sg: e.matmul(
                                    pg, wgb[ws].rearrange("p (c m) -> p c m", c=NCC)[:, cc, ts_(m4)],
                                    xr[:, cc, sg * 512:(sg + 1) * 512],
                                    start=(cc == 0), stop=(cc == NCC - 1)), "sPE", 1 if cc == NCC - 1 else None)
                            gmm = em.v("sPE")
                            em.wait("scalar", "sPE", gmm)
                            if hmul_mark[sl2]:
                                em.wait("scalar", "sVE", hmul_mark[sl2])
                            em.op("scalar", lambda e, pg=pg, sl2=sl2: e.activation(
                                s_sb[sl2][:, :], pg, AF.Silu), "sAC", 1)
                            silu_mark[sl2] = em.v("sAC")
                            for cc in range(NCC):
                                em.op("tensor", lambda e, pu=pu, ws=ws, cc=cc, m4=m4, sg=sg: e.matmul(
                                    pu, wub[ws].rearrange("p (c m) -> p c m", c=NCC)[:, cc, ts_(m4)],
                                    xr[:, cc, sg * 512:(sg + 1) * 512],
                                    start=(cc == 0), stop=(cc == NCC - 1)), "sPE", 1 if cc == NCC - 1 else None)
                            umm = em.v("sPE")
                            em.wait("vector", "sPE", umm)
                            em.wait("vector", "sAC", silu_mark[sl2])
                            em.op("vector", lambda e, pu=pu, m4=m4, sl2=sl2: e.tensor_tensor(
                                h_sb[:, ts_(m4, 512)], s_sb[sl2][:, :], pu, op=ALU.mult), "sVE", 1)
                            hmul_mark[sl2] = em.v("sVE")
                        h_done = em.v("sVE")
                        em.wait("tensor", "sVE", h_done)
                        for ti in range(4):
                            gtile = sg * 4 + ti
                            for ch in range(2):
                                dslot = (ti * 2 + ch) % 4
                                pd = (P2 if dslot < 2 else P3).bitcast(F32)[:, (dslot % 2) * 512:(dslot % 2) * 512 + 512]
                                if drot_mark[dslot]:
                                    em.wait("tensor", "sVE", drot_mark[dslot])
                                for m4 in range(4):
                                    em.op("tensor", lambda e, pd=pd, m4=m4, ti=ti, ws=ws, ch=ch: e.matmul(
                                        pd, h_sb[:, m4 * 512 + ti * 128: m4 * 512 + (ti + 1) * 128],
                                        wdb[ws].rearrange("p (m c) -> p m c", m=4)[:, m4, ch * 512:(ch + 1) * 512],
                                        start=(m4 == 0), stop=(m4 == 3)), "sPE", 1 if m4 == 3 else None)
                                dmm = em.v("sPE")
                                em.wait("vector", "sPE", dmm)
                                if (gtile, ch) in y_sl_mark:
                                    em.wait("vector", "sVE", y_sl_mark[(gtile, ch)])
                                sc_ap = ce_col if ce_col is not None else comb[:, gtile * E + ei: gtile * E + ei + 1]
                                em.op("vector", lambda e, pd=pd, gtile=gtile, ch=ch, sc_ap=sc_ap: e.scalar_tensor_tensor(
                                    y_ap[:, gtile, ch * 512:(ch + 1) * 512], pd, sc_ap,
                                    y_ap[:, gtile, ch * 512:(ch + 1) * 512],
                                    op0=ALU.mult, op1=ALU.add), "sVE", 1)
                                drot_mark[dslot] = em.v("sVE")
                                y_sl_mark[(gtile, ch)] = drot_mark[dslot]
                    wslot_free[ws] = em.v("sPE")
                y_done = em.v("sVE")
                em.wait("sync", "sVE", y_done)
                em.op("sync", lambda e, row0=row0: e.dma_start(
                    dst[row0:row0 + TH, :].rearrange("(t p) c -> p t c", p=128), y_ap), "sSY", 16)

        def prep_attn_b():
            em.wait("gpsimd", "sSY")
            em.wait("tensor", "sVE")
            em.wait("tensor", "sAC")
            stage = B_view(0, 1024)
            spill_mark = 0
            for (Wt, dstd) in [(bWq, wq_d), (bWk, wk_d), (bWv, wv_d), (bWo, wp_d)]:
                if spill_mark:
                    em.wait("vector", "sSY", spill_mark)
                for rr in range(NCC):
                    em.op("gpsimd", lambda e, Wt=Wt, rr=rr: e.dma_start(stage[:, :], Wt[ts_(rr), :]), "sGP", 16)
                    gpm = em.v("sGP")
                    em.wait("tensor", "sGP", gpm)
                    for cc in range(NCC):
                        em.op("tensor", lambda e, cc=cc: e.matmul(
                            P2.bitcast(BF16)[:, cc * 128:(cc + 1) * 128],
                            stage[:, ts_(cc)], ident[:, :],
                            start=(cc == 0), stop=(cc == NCC - 1), is_transpose=True),
                            "sPE", 1 if cc == NCC - 1 else None)
                    trm = em.v("sPE")
                    em.wait("vector", "sPE", trm)
                    dst2 = W2[:, :].rearrange("p (c n) -> p c n", c=NCC)[:, :, rr * 128:(rr + 1) * 128]
                    em.op("vector", lambda e, dst2=dst2: e.tensor_copy(
                        dst2, P2.bitcast(BF16)[:, 0:1024].rearrange("p (c n) -> p c n", c=NCC)), "sVE", 1)
                    vem = em.v("sVE")
                    em.wait("gpsimd", "sVE", vem)
                em.wait("sync", "sVE", em.v("sVE"))
                em.op("sync", lambda e, dstd=dstd: e.dma_start(
                    dstd.rearrange("(cc p) n -> p cc n", p=128),
                    W2[:, :].rearrange("p (c n) -> p c n", c=NCC)), "sSY", 16)
                spill_mark = em.v("sSY")

        emit_attn("a", x_in, y_out if n_sub == 1 else res_a)
        if n_sub >= 2:
            emit_ffn(res_a, y_out if n_sub == 2 else res_b)
        if n_sub >= 3:
            prep_attn_b()
            emit_attn("b", res_b, y_out if n_sub == 3 else res_a)
        if n_sub >= 4:
            emit_ffn(res_a, y_out)

        em.wait("sync", "sSY")
        em.wait("sync", "sGP")

        @block.sync
        def _(e):
            for f in em.prog["sync"]:
                f(e)

        @block.gpsimd
        def _(e):
            for f in em.prog["gpsimd"]:
                f(e)

        @block.tensor
        def _(e):
            for f in em.prog["tensor"]:
                f(e)

        @block.scalar
        def _(e):
            for f in em.prog["scalar"]:
                f(e)

        @block.vector
        def _(e):
            for f in em.prog["vector"]:
                f(e)

    return nc


def host_constants():
    ident = np.eye(128, dtype=ml_dtypes.bfloat16)
    ident_f = np.eye(128, dtype=np.float32)
    ones_bf = np.ones((128, 128), dtype=ml_dtypes.bfloat16)
    ones_f32 = np.ones((1, 128), dtype=np.float32)
    masks = np.zeros((4, 128, 512), dtype=ml_dtypes.bfloat16)
    for d in range(4):
        ki = np.arange(128)[:, None]
        qi = np.arange(512)[None, :]
        masks[d] = np.where(128 * d + ki <= qi, 0.0, -1e30).astype(ml_dtypes.bfloat16)
    return {"c_ident": ident, "c_ident_f": ident_f, "c_ones_bf": ones_bf,
            "c_ones_f32": ones_f32, "c_masks": masks}


def shard_inputs(inputs, n_cores=8):
    w = {
        "aWq": inputs["L0_attn_a_Wq"], "aWk": inputs["L0_attn_a_Wk"], "aWv": inputs["L0_attn_a_Wv"],
        "aAq": inputs["L0_attn_a_Aq"], "aAk": inputs["L0_attn_a_Ak"], "aAv": inputs["L0_attn_a_Av"],
        "aBq": inputs["L0_attn_a_Bq"], "aBk": inputs["L0_attn_a_Bk"], "aBv": inputs["L0_attn_a_Bv"],
        "aWp": inputs["L0_attn_a_Wp"], "abp": inputs["L0_attn_a_bp"],
        "aAp": inputs["L0_attn_a_Ap"], "aBp": inputs["L0_attn_a_Bp"],
        "bWq": inputs["L0_attn_b_Wq"], "bWk": inputs["L0_attn_b_Wk"],
        "bWv": inputs["L0_attn_b_Wv"], "bWo": inputs["L0_attn_b_Wo"],
        "bbq": inputs["L0_attn_b_bq"], "bbk": inputs["L0_attn_b_bk"],
        "bbv": inputs["L0_attn_b_bv"], "bbo": inputs["L0_attn_b_bo"],
        "bthq": inputs["L0_attn_b_th_q"], "bthk": inputs["L0_attn_b_th_k"],
        "gk": inputs["L0_ffn_gate_kernel"], "gb": inputs["L0_ffn_gate_bias"],
        "wg": inputs["L0_ffn_w_gate"], "wu": inputs["L0_ffn_w_up"], "wd": inputs["L0_ffn_w_down"],
        "swg": inputs["L0_ffn_sw_gate"], "swu": inputs["L0_ffn_sw_up"], "swd": inputs["L0_ffn_sw_down"],
    }
    w = {k: np.ascontiguousarray(np.asarray(v, dtype=np.float32)) for k, v in w.items()}
    consts = host_constants()
    x = np.ascontiguousarray(np.asarray(inputs["L0_attn_a_x"], dtype=np.float32))
    maps = []
    for c in range(n_cores):
        m = {"x": np.ascontiguousarray(x[c])}
        m.update(w)
        m.update(consts)
        maps.append(m)
    return maps


def kernel(**inputs):
    n_cores = 8
    maps = shard_inputs(inputs, n_cores)
    nc = build_core(4096)
    res = run_bass_kernel_spmd(nc, maps, core_ids=list(range(n_cores)))
    return np.stack([np.asarray(r["y"], dtype=np.float32) for r in res.results], axis=0)


# version 06
# speedup vs baseline: 1.0379x; 1.0379x over previous
import numpy as np
import ml_dtypes
from contextlib import ExitStack

import concourse.bass as bass
import concourse.mybir as mybir
from concourse.bass_utils import run_bass_kernel_spmd

F32 = mybir.dt.float32
BF16 = mybir.dt.bfloat16
AF = mybir.ActivationFunctionType
ALU = mybir.AluOpType
AX = mybir.AxisListType

C = 1024
H = 16
HD = 64
NCC = C // 128
NHP = H // 2
R = 8
SCALING = 16.0 / R
E = 16
M = 512
EPS = 1e-5
EXP_SHIFT = -12.0

ENGS = ("sync", "gpsimd", "tensor", "scalar", "vector")


class Em:
    def __init__(self):
        self.prog = {k: [] for k in ENGS}
        self.cnt = {}
        self.h = {}

    def add_sem(self, name, handle):
        self.h[name] = handle
        self.cnt[name] = 0

    def wait(self, eng, sem, val=None):
        if val is None:
            val = self.cnt[sem]
        assert val <= self.cnt[sem], f"wait on future count {sem} {val} > {self.cnt[sem]}"
        if val <= 0:
            return
        h = self.h[sem]
        self.prog[eng].append(lambda e, h=h, v=val: e.wait_ge(h, v))

    def op(self, eng, build, sem=None, n=1):
        if sem is None or n is None:
            self.prog[eng].append(build)
            return None
        h = self.h[sem]
        self.cnt[sem] += n
        self.prog[eng].append(lambda e, b=build, h=h, n=n: b(e).then_inc(h, n))
        return self.cnt[sem]

    def v(self, sem):
        return self.cnt[sem]

    def vo(self, build):
        self.wait("vector", "sVE")
        return self.op("vector", build, "sVE", 1)


def ts_(i, n=128):
    return slice(i * n, (i + 1) * n)


def build_core(T=4096, n_sub=4, dbg=False):
    NT = T // 128
    NQG = T // 512
    TH = T // 2
    NTH = TH // 128
    NSG = TH // 512

    nc = bass.Bass("TRN2", target_bir_lowering=False, detect_race_conditions=False)

    def din(name, shape, dtype=F32):
        return nc.dram_tensor(name, list(shape), dtype, kind="ExternalInput").ap()

    x_in = din("x", (T, C))
    aWq = din("aWq", (H, C, HD)); aWk = din("aWk", (H, C, HD)); aWv = din("aWv", (H, C, HD))
    aAq = din("aAq", (H, C, R)); aAk = din("aAk", (H, C, R)); aAv = din("aAv", (H, C, R))
    aBq = din("aBq", (H, R, HD)); aBk = din("aBk", (H, R, HD)); aBv = din("aBv", (H, R, HD))
    aWp = din("aWp", (C, C)); abp = din("abp", (C,))
    aAp = din("aAp", (C, R)); aBp = din("aBp", (R, C))
    bWq = din("bWq", (C, C)); bWk = din("bWk", (C, C)); bWv = din("bWv", (C, C)); bWo = din("bWo", (C, C))
    bbq = din("bbq", (C,)); bbk = din("bbk", (C,)); bbv = din("bbv", (C,)); bbo = din("bbo", (C,))
    bthq = din("bthq", (C,)); bthk = din("bthk", (C,))
    gkin = din("gk", (C, E)); gbin = din("gb", (E,))
    wg = din("wg", (E, C, M)); wu = din("wu", (E, C, M)); wd = din("wd", (E, M, C))
    swg = din("swg", (C, 2 * M)); swu = din("swu", (C, 2 * M)); swd = din("swd", (2 * M, C))
    c_ident = din("c_ident", (128, 128), BF16)
    c_ones_bf = din("c_ones_bf", (128, 128), BF16)
    c_ones_f32 = din("c_ones_f32", (1, 128), F32)
    c_masks = din("c_masks", (4, 128, 512), BF16)
    c_ident_f = din("c_ident_f", (128, 128), F32)

    y_out = nc.dram_tensor("y", [T, C], F32, kind="ExternalOutput").ap()
    res_a = nc.dram_tensor("res_a", [T, C], F32, kind="Internal").ap()
    res_b = nc.dram_tensor("res_b", [T, C], F32, kind="Internal").ap()
    kind_dbg = "ExternalOutput" if dbg else "Internal"
    qt_d = nc.dram_tensor("qt_d", [C, T], BF16, kind=kind_dbg).ap()
    kt_d = nc.dram_tensor("kt_d", [C, T], BF16, kind=kind_dbg).ap()
    wq_d = nc.dram_tensor("wq_d", [C, C], BF16, kind="Internal").ap()
    wk_d = nc.dram_tensor("wk_d", [C, C], BF16, kind="Internal").ap()
    wv_d = nc.dram_tensor("wv_d", [C, C], BF16, kind="Internal").ap()
    wp_d = nc.dram_tensor("wp_d", [C, C], BF16, kind="Internal").ap()
    xlt_dbg = nc.dram_tensor("xlt_dbg", [C, T], BF16, kind=kind_dbg).ap() if dbg else None
    ot_dbg = nc.dram_tensor("ot_dbg", [C, T], BF16, kind=kind_dbg).ap() if dbg else None
    iz_dbg = nc.dram_tensor("iz_dbg", [2, 512], F32, kind=kind_dbg).ap() if dbg else None
    p_dbg = nc.dram_tensor("p_dbg", [128, 4096], BF16, kind=kind_dbg).ap() if dbg else None
    n_dbg = nc.dram_tensor("n_dbg", [128, 1024], BF16, kind=kind_dbg).ap() if dbg else None
    bc_dbg = nc.dram_tensor("bc_dbg", [128, 1024], F32, kind=kind_dbg).ap() if dbg else None

    em = Em()

    with ExitStack() as cx:
        e_ = cx.enter_context

        A_cols = max(8 * T, 4 * T + 16384)
        A = e_(nc.sbuf_tensor("arenaA", [128, A_cols], BF16))
        B = e_(nc.sbuf_tensor("arenaB", [128, 8 * T], BF16))
        Ca = e_(nc.sbuf_tensor("arenaC", [128, 24576], BF16))
        W2 = e_(nc.sbuf_tensor("W2", [128, NCC * C], BF16))
        ident = e_(nc.sbuf_tensor("ident", [128, 128], BF16))
        ident_f = e_(nc.sbuf_tensor("ident_f", [128, 128], F32))
        ones_bf = e_(nc.sbuf_tensor("ones_bf", [128, 128], BF16))
        ones_f32 = e_(nc.sbuf_tensor("ones_f32", [1, 128], F32))
        ones_col = e_(nc.sbuf_tensor("ones_col", [128, 1], BF16))
        one_col_f = e_(nc.sbuf_tensor("one_col_f", [128, 1], F32))
        nb_sh = e_(nc.sbuf_tensor("nb_sh", [128, 1], F32))
        eps_t = e_(nc.sbuf_tensor("eps_t", [128, 1], F32))
        lnsc = e_(nc.sbuf_tensor("lnsc", [128, 16], F32))
        qt_stage = e_(nc.sbuf_tensor("qt_stage", [128, 4 * 512], BF16))
        junk_bf = qt_stage[:, 0:C]
        biasrows = e_(nc.sbuf_tensor("biasrows", [128, C], BF16))
        bqv = e_(nc.sbuf_tensor("bqv", [128, 8], F32))
        bkv = e_(nc.sbuf_tensor("bkv", [128, 8], F32))
        thqv = e_(nc.sbuf_tensor("thqv", [128, 8], F32))
        thkv = e_(nc.sbuf_tensor("thkv", [128, 8], F32))
        comb = e_(nc.sbuf_tensor("comb", [128, 16 * E], F32))
        gk_sb = e_(nc.sbuf_tensor("gk_sb", [128, NCC * E], BF16))
        gb_bc = e_(nc.sbuf_tensor("gb_bc", [128, E], F32))
        gb_row = e_(nc.sbuf_tensor("gb_row", [1, E], F32))
        gsc = e_(nc.sbuf_tensor("gsc", [128, 128], F32))

        P0 = e_(nc.psum_tensor("P0", [128, 1024], F32))
        P1 = e_(nc.psum_tensor("P1", [128, 1024], F32))
        P2 = e_(nc.psum_tensor("P2", [128, 1024], F32))
        P3 = e_(nc.psum_tensor("P3", [128, 1024], F32))

        for s in ("sSY", "sGP", "sPE", "sAC", "sVE"):
            em.add_sem(s, e_(nc.semaphore(s)))

        block = e_(nc.Block())

        def A_view(c0, n, dt=BF16):
            ap = A[:, c0:c0 + n]
            return ap.bitcast(dt) if dt is not BF16 else ap

        def B_view(c0, n, dt=BF16):
            ap = B[:, c0:c0 + n]
            return ap.bitcast(dt) if dt is not BF16 else ap

        def C_view(c0, n, dt=BF16):
            ap = Ca[:, c0:c0 + n]
            return ap.bitcast(dt) if dt is not BF16 else ap

        em.op("sync", lambda e: e.dma_start(ident[:, :], c_ident), "sSY", 16)
        em.op("sync", lambda e: e.dma_start(ident_f[:, :], c_ident_f), "sSY", 16)
        em.op("sync", lambda e: e.dma_start(ones_bf[:, :], c_ones_bf), "sSY", 16)
        em.op("sync", lambda e: e.dma_start(ones_f32[:, :], c_ones_f32), "sSY", 16)
        em.op("sync", lambda e: e.dma_start(gb_row[:, :], gbin[None, :]), "sSY", 16)
        for j in range(NCC):
            em.op("sync", lambda e, j=j: e.dma_start(bqv[:, j:j + 1], bbq[ts_(j)][:, None]), "sSY", 16)
            em.op("sync", lambda e, j=j: e.dma_start(bkv[:, j:j + 1], bbk[ts_(j)][:, None]), "sSY", 16)
            em.op("sync", lambda e, j=j: e.dma_start(thqv[:, j:j + 1], bthq[ts_(j)][:, None]), "sSY", 16)
            em.op("sync", lambda e, j=j: e.dma_start(thkv[:, j:j + 1], bthk[ts_(j)][:, None]), "sSY", 16)
        em.op("gpsimd", lambda e: e.dma_start(biasrows[0:1, :], abp[None, :]), "sGP", 16)
        em.op("gpsimd", lambda e: e.dma_start(biasrows[32:33, :], bbv[None, :]), "sGP", 16)
        em.op("gpsimd", lambda e: e.dma_start(biasrows[64:65, :], bbo[None, :]), "sGP", 16)
        em.op("gpsimd", lambda e: e.dma_start(
            gk_sb[:, :].rearrange("p (c e) -> p c e", c=NCC),
            gkin.rearrange("(cc p) e -> p cc e", p=128)), "sGP", 16)
        em.op("vector", lambda e: e.memset(ones_col[:, :], 1.0))
        em.op("vector", lambda e: e.memset(one_col_f[:, :], 1.0))
        em.op("vector", lambda e: e.memset(nb_sh[:, :], EXP_SHIFT))
        em.op("vector", lambda e: e.memset(eps_t[:, :], EPS))
        em.op("vector", lambda e: e.memset(gsc[:, 124:128], 0.0), "sVE", 1)
        em.wait("tensor", "sSY")
        em.op("tensor", lambda e: e.matmul(P3.bitcast(F32)[:, 0:E], ones_f32[:, :], gb_row[:, :],
                                           start=True, stop=True), "sPE", 1)
        em.wait("vector", "sPE")
        em.op("vector", lambda e: e.tensor_copy(gb_bc[:, :], P3.bitcast(F32)[:, 0:E]), "sVE", 1)

        def fold_attn_a():
            a_stage = B_view(0, 2048, F32)
            bpr = Ca[0:8, 0:C]
            aqt = Ca[0:8, C:2 * C]
            ve_guard = 0
            spill_mark = 0
            for (Wt, At, Bt, dstd) in [(aWq, aAq, aBq, wq_d), (aWk, aAk, aBk, wk_d),
                                       (aWv, aAv, aBv, wv_d)]:
                dst = W2[:, :]
                if spill_mark:
                    em.wait("gpsimd", "sSY", spill_mark)
                for cc in range(NCC):
                    em.op("gpsimd", lambda e, Wt=Wt, dst=dst, cc=cc: e.dma_start(
                        dst[:, ts_(cc, C)].rearrange("p (h d) -> p h d", h=H),
                        Wt[:, ts_(cc), :].rearrange("h p d -> p h d")), "sGP", 16)
                if ve_guard:
                    em.wait("sync", "sVE", ve_guard)
                    em.wait("gpsimd", "sVE", ve_guard)
                em.op("sync", lambda e, At=At: e.dma_start(
                    a_stage.rearrange("p (h c r) -> p h c r", h=H, c=NCC),
                    At.rearrange("h (cc p) r -> p h cc r", p=128)), "sSY", 16)
                em.op("gpsimd", lambda e, Bt=Bt: e.dma_start(
                    bpr[:, :].rearrange("r (h d) -> r h d", h=H),
                    Bt.rearrange("h r d -> r h d")), "sGP", 16)
                base_gp = em.v("sGP")
                base_sy = em.v("sSY")
                for h in range(H):
                    em.wait("tensor", "sSY", base_sy)
                    if ve_guard:
                        em.wait("tensor", "sVE", ve_guard)
                    for cc in range(NCC):
                        src = a_stage.rearrange("p (h c r) -> p h c r", h=H, c=NCC)[:, h, cc, :]
                        em.op("tensor", lambda e, src=src, cc=cc: e.matmul(
                            P0.bitcast(F32)[0:8, ts_(cc)], src, ident_f[:, :],
                            start=(cc % 4 == 0), stop=(cc % 4 == 3), is_transpose=True),
                            "sPE", 1 if cc == NCC - 1 else None)
                    pe_tr = em.v("sPE")
                    em.wait("vector", "sPE", pe_tr)
                    em.op("vector", lambda e: e.tensor_scalar_mul(
                        aqt[0:8, :], P0.bitcast(F32)[0:8, :], SCALING), "sVE", 1)
                    ve_aqt = em.v("sVE")
                    em.wait("tensor", "sVE", ve_aqt)
                    em.wait("tensor", "sGP", base_gp)
                    for cc in range(NCC):
                        em.op("tensor", lambda e, cc=cc, h=h: e.matmul(
                            P1.bitcast(F32)[:, cc * HD:(cc + 1) * HD],
                            aqt[0:8, ts_(cc)], bpr[0:8, h * HD:(h + 1) * HD],
                            start=(cc == 0), stop=(cc == NCC - 1)), "sPE", 1 if cc == NCC - 1 else None)
                    pe_mm = em.v("sPE")
                    em.wait("vector", "sPE", pe_mm)
                    dstv = dst.rearrange("p (c hd) -> p c hd", c=NCC)
                    lora = P1.bitcast(F32)[:, 0:NCC * HD].rearrange("p (c hd) -> p c hd", c=NCC)
                    em.op("vector", lambda e, dstv=dstv, lora=lora, h=h: e.tensor_tensor(
                        dstv[:, :, h * HD:(h + 1) * HD], dstv[:, :, h * HD:(h + 1) * HD],
                        lora, op=ALU.add), "sVE", 1)
                    ve_guard = em.v("sVE")
                em.wait("sync", "sVE", ve_guard)
                em.op("sync", lambda e, dstd=dstd: e.dma_start(
                    dstd.rearrange("(cc p) n -> p cc n", p=128),
                    W2[:, :].rearrange("p (c n) -> p c n", c=NCC)), "sSY", 16)
                spill_mark = em.v("sSY")
            em.wait("gpsimd", "sSY", spill_mark)
            for j in range(NCC):
                em.op("gpsimd", lambda e, j=j: e.dma_start(W2[:, ts_(j, C)], aWp[ts_(j), :]), "sGP", 16)
            em.wait("sync", "sVE", ve_guard)
            em.wait("gpsimd", "sVE", ve_guard)
            em.op("sync", lambda e: e.dma_start(
                a_stage.rearrange("p (g r) -> p g r", g=128)[:, 0:NCC, :],
                aAp.rearrange("(cc p) r -> p cc r", p=128)), "sSY", 16)
            em.op("gpsimd", lambda e: e.dma_start(bpr[:, 0:C], aBp), "sGP", 16)
            sy2 = em.v("sSY")
            gp2 = em.v("sGP")
            em.wait("tensor", "sSY", sy2)
            em.wait("tensor", "sVE", ve_guard)
            for cc in range(NCC):
                src = a_stage.rearrange("p (g r) -> p g r", g=128)[:, cc, :]
                em.op("tensor", lambda e, src=src, cc=cc: e.matmul(
                    P0.bitcast(F32)[0:8, ts_(cc)], src, ident_f[:, :],
                    start=(cc % 4 == 0), stop=(cc % 4 == 3), is_transpose=True),
                    "sPE", 1 if cc == NCC - 1 else None)
            em.wait("vector", "sPE")
            em.op("vector", lambda e: e.tensor_scalar_mul(aqt[0:8, :], P0.bitcast(F32)[0:8, :], SCALING), "sVE", 1)
            em.wait("tensor", "sVE")
            em.wait("tensor", "sGP", gp2)
            add_prev = 0
            for cc in range(NCC):
                if add_prev:
                    em.wait("tensor", "sVE", add_prev)
                for nh in range(2):
                    em.op("tensor", lambda e, cc=cc, nh=nh: e.matmul(
                        P1.bitcast(F32)[:, ts_(nh, 512)],
                        aqt[0:8, ts_(cc)], bpr[0:8, ts_(nh, 512)],
                        start=True, stop=True), "sPE", 1 if nh == 1 else None)
                pe_mm = em.v("sPE")
                em.wait("vector", "sPE", pe_mm)
                em.op("vector", lambda e, cc=cc: e.tensor_tensor(
                    W2[:, ts_(cc, C)], W2[:, ts_(cc, C)], P1.bitcast(F32)[:, 0:C],
                    op=ALU.add), "sVE", 1)
                add_prev = em.v("sVE")

            em.wait("sync", "sVE", em.v("sVE"))
            em.op("sync", lambda e: e.dma_start(
                wp_d.rearrange("(cc p) n -> p cc n", p=128),
                W2[:, :].rearrange("p (c n) -> p c n", c=NCC)), "sSY", 16)

        fold_attn_a()

        def emit_ln(src, row0, n_tiles, xlnT, xstg, lnstg):
            em.wait("sync", "sSY")
            em.wait("sync", "sPE")
            pre_pe = em.v("sPE")
            pre_ve = em.v("sVE")
            tr_mark = [0] * n_tiles
            ev_mark = [0] * n_tiles
            xl_mark = [0] * n_tiles
            var_mark = [0] * n_tiles
            for i in range(n_tiles):
                s = i % 2
                if i >= 2:
                    em.wait("sync", "sVE", xl_mark[i - 2])
                em.op("sync", lambda e, i=i, s=s: e.dma_start(
                    xstg[s], src[row0 + i * 128: row0 + (i + 1) * 128, :]), "sSY", 16)
                ld = em.v("sSY")
                em.wait("scalar", "sSY", ld)
                if i >= 2:
                    em.wait("scalar", "sVE", var_mark[i - 2])
                em.op("scalar", lambda e, s=s: e.activation(
                    junk_bf[:, :], xstg[s], AF.Square, accum_out=lnsc[:, 2 + s:3 + s]), "sAC", 1)
                em.op("scalar", lambda e, s=s: e.activation(
                    junk_bf[:, :], xstg[s], AF.Copy, accum_out=lnsc[:, 0 + s:1 + s]), "sAC", 1)
                sqm = em.v("sAC")
                em.wait("vector", "sAC", sqm)
                em.vo(lambda e, s=s: e.tensor_scalar_mul(lnsc[:, 4 + s:5 + s], lnsc[:, 0 + s:1 + s], 1.0 / C))
                em.vo(lambda e, s=s: e.tensor_tensor(
                    lnsc[:, 6 + s:7 + s], lnsc[:, 4 + s:5 + s], lnsc[:, 4 + s:5 + s], op=ALU.mult))
                em.vo(lambda e, s=s: e.scalar_tensor_tensor(
                    lnsc[:, 8 + s:9 + s], lnsc[:, 2 + s:3 + s], 1.0 / C, lnsc[:, 6 + s:7 + s],
                    op0=ALU.mult, op1=ALU.subtract))
                vm = em.v("sVE")
                var_mark[i] = vm
                em.wait("scalar", "sVE", vm)
                em.op("scalar", lambda e, s=s: e.activation(
                    lnsc[:, 10 + s:11 + s], lnsc[:, 8 + s:9 + s], AF.Sqrt, bias=eps_t[:, :]), "sAC", 1)
                sq2 = em.v("sAC")
                em.wait("vector", "sAC", sq2)
                em.vo(lambda e, s=s: e.reciprocal(lnsc[:, 12 + s:13 + s], lnsc[:, 10 + s:11 + s]))
                if i >= 2:
                    em.wait("vector", "sPE", tr_mark[i - 2])
                em.vo(lambda e, s=s: e.tensor_scalar(
                    lnstg[s], xstg[s], lnsc[:, 4 + s:5 + s], lnsc[:, 12 + s:13 + s],
                    op0=ALU.subtract, op1=ALU.mult))
                xl_mark[i] = em.v("sVE")
                ptr = P2 if s == 0 else P3
                em.wait("tensor", "sVE", xl_mark[i])
                if i >= 2:
                    em.wait("tensor", "sVE", ev_mark[i - 2])
                for cc in range(NCC):
                    em.op("tensor", lambda e, s=s, cc=cc, ptr=ptr: e.matmul(
                        ptr.bitcast(F32)[:, ts_(cc)], lnstg[s][:, ts_(cc)], ident_f[:, :],
                        start=(cc % 4 == 0), stop=(cc % 4 == 3), is_transpose=True),
                        "sPE", 1 if cc == NCC - 1 else None)
                tr_mark[i] = em.v("sPE")
                em.wait("vector", "sPE", tr_mark[i])
                dstv = xlnT.rearrange("p (c t) -> p c t", c=NCC)[:, :, i * 128:(i + 1) * 128]
                em.op("vector", lambda e, dstv=dstv, ptr=ptr: e.tensor_copy(
                    dstv, ptr.bitcast(F32)[:, :].rearrange("p (c t) -> p c t", c=NCC)), "sVE", 1)
                ev_mark[i] = em.v("sVE")

        def emit_attn(variant, src, dst):
            is_a = variant == "a"
            xlnT = A_view(0, NCC * T)
            xstg = [B_view(k * 2048, 2048, F32) for k in range(2)]
            lnstg = [B_view(4096 + k * 2048, 2048, F32) for k in range(2)]
            emit_ln(src, 0, NT, xlnT, xstg, lnstg)
            xr = xlnT.rearrange("p (c t) -> p c t", c=NCC)
            if dbg == 2 and is_a:
                em.wait("sync", "sVE")
                for ccd in range(NCC):
                    em.op("sync", lambda e, ccd=ccd: e.dma_start(
                        xlt_dbg[ts_(ccd)], xr[:, ccd, :]), "sSY", 16)
                em.wait("tensor", "sSY")

            em.wait("sync", "sPE")
            if is_a:
                for mi in range(4):
                    em.op("sync", lambda e, mi=mi: e.dma_start(
                        C_view(18432 + mi * 512, 512), c_masks[mi]), "sSY", 16)
            wA = W2[:, :]
            wB = C_view(0, NCC * C)
            srcs = {"q": wq_d, "k": wk_d, "v": wv_d, "p": wp_d}
            em.op("sync", lambda e: e.dma_start(
                wA.rearrange("p (c n) -> p c n", c=NCC),
                srcs["q"].rearrange("(cc p) n -> p cc n", p=128)), "sSY", 16)
            em.op("sync", lambda e: e.dma_start(
                wB.rearrange("p (c n) -> p c n", c=NCC),
                srcs["k"].rearrange("(cc p) n -> p cc n", p=128)), "sSY", 16)
            wqk_ld = em.v("sSY")
            mask_load = em.v("sSY")
            qkv_entry_ve = em.v("sVE")
            qkv_entry_ac = em.v("sAC")
            em.wait("tensor", "sVE", qkv_entry_ve)
            em.wait("tensor", "sAC", qkv_entry_ac)
            em.wait("tensor", "sSY", wqk_ld)
            qs_mark = [0, 0, 0, 0]
            ev_rot = [0, 0, 0, 0]
            qk_spill = [[0] * NCC, [0] * NCC]
            qt_pe_done = 0
            rot = 0
            for which, (Wsb, dstd, bvec, thvec) in enumerate(
                    [(wA, qt_d, bqv, thqv), (wB, kt_d, bkv, thkv)]):
                if which == 1:
                    qt_pe_done = em.v("sPE")
                    em.wait("sync", "sPE", qt_pe_done)
                    em.op("sync", lambda e: e.dma_start(
                        wA.rearrange("p (c n) -> p c n", c=NCC),
                        srcs["v"].rearrange("(cc p) n -> p cc n", p=128)), "sSY", 16)
                for j in range(NCC):
                    for qg in range(NQG):
                        pq = (P0 if rot < 2 else P1).bitcast(F32)[:, (rot % 2) * 512:(rot % 2) * 512 + 512]
                        if ev_rot[rot]:
                            em.wait("tensor", "sAC" if is_a else "sVE", ev_rot[rot])
                        for cc in range(NCC):
                            em.op("tensor", lambda e, pq=pq, Wsb=Wsb, cc=cc, j=j, qg=qg: e.matmul(
                                pq, Wsb[:, cc * C + j * 128: cc * C + (j + 1) * 128],
                                xr[:, cc, qg * 512:(qg + 1) * 512],
                                start=(cc == 0), stop=(cc == NCC - 1)),
                                "sPE", 1 if cc == NCC - 1 else None)
                        mm_mark = em.v("sPE")
                        sslot = rot
                        if is_a:
                            em.wait("scalar", "sPE", mm_mark)
                            if qs_mark[sslot]:
                                em.wait("scalar", "sSY", qs_mark[sslot])
                            em.op("scalar", lambda e, pq=pq, sslot=sslot: e.copy(
                                qt_stage[:, ts_(sslot, 512)], pq), "sAC", 1)
                            ev_rot[rot] = em.v("sAC")
                            em.wait("sync", "sAC", ev_rot[rot])
                        else:
                            em.wait("vector", "sPE", mm_mark)
                            if qs_mark[sslot]:
                                em.wait("vector", "sSY", qs_mark[sslot])
                            em.op("vector", lambda e, pq=pq, sslot=sslot, bvec=bvec, thvec=thvec, j=j: e.tensor_scalar(
                                qt_stage[:, ts_(sslot, 512)], pq, bvec[:, j:j + 1], thvec[:, j:j + 1],
                                op0=ALU.add, op1=ALU.is_gt), "sVE", 1)
                            ev_rot[rot] = em.v("sVE")
                            em.wait("sync", "sVE", ev_rot[rot])
                        em.op("sync", lambda e, dstd=dstd, j=j, qg=qg, sslot=sslot: e.dma_start(
                            dstd[ts_(j), qg * 512:(qg + 1) * 512], qt_stage[:, ts_(sslot, 512)]), "sSY", 16)
                        qs_mark[sslot] = em.v("sSY")
                        rot = (rot + 1) % 4
                    qk_spill[which][j] = em.v("sSY")
            wv_ld = em.v("sSY")
            em.wait("tensor", "sSY", wv_ld)
            v_sb = B_view(0, 8 * T)
            vv = v_sb.rearrange("p (t hd) -> p t hd", t=NT)
            vrot_mark = [0, 0]
            for i in range(NT):
                for hh in range(2):
                    pv = (P2 if hh == 0 else P3).bitcast(F32)[:, 0:512]
                    if vrot_mark[hh]:
                        em.wait("tensor", "sAC", vrot_mark[hh])
                    for cc in range(NCC):
                        last = (cc == NCC - 1) and is_a
                        em.op("tensor", lambda e, pv=pv, cc=cc, i=i, hh=hh, last=last: e.matmul(
                            pv, xr[:, cc, i * 128:(i + 1) * 128],
                            wA[:, cc * C + hh * 512: cc * C + hh * 512 + 512],
                            start=(cc == 0), stop=last), "sPE", 1 if last else None)
                    if not is_a:
                        em.op("tensor", lambda e, pv=pv, hh=hh: e.matmul(
                            pv, ones_bf[32:33, :], biasrows[32:33, hh * 512:hh * 512 + 512],
                            start=False, stop=True), "sPE", 1)
                    mmv = em.v("sPE")
                    em.wait("scalar", "sPE", mmv)
                    em.op("scalar", lambda e, pv=pv, i=i, hh=hh: e.copy(
                        v_sb[:, i * C + hh * 512: i * C + hh * 512 + 512], pv), "sAC", 1)
                    vrot_mark[hh] = em.v("sAC")
            v_done = em.v("sAC")
            v_pe_done = em.v("sPE")
            em.wait("sync", "sPE", v_pe_done)
            em.op("sync", lambda e: e.dma_start(
                wA.rearrange("p (c n) -> p c n", c=NCC),
                srcs["p"].rearrange("(cc p) n -> p cc n", p=128)), "sSY", 16)
            wp_ld = em.v("sSY")

            otF = A_view(0, NCC * T)
            otc = otF.rearrange("p (k t) -> p k t", k=NHP)
            qkt = [(C_view(0, T), C_view(T, T)), (C_view(2 * T, T), C_view(3 * T, T))]
            p_sb = [C_view(4 * T + k * 1024, 1024) for k in range(2)]
            mask_t = [C_view(18432 + mi * 512, 512) for mi in range(4)]
            ot_tmp = C_view(20480, 1024)
            invz = C_view(21504, 1024)
            invzf = C_view(22528, 2048, F32)

            em.wait("scalar", "sVE", 5)
            em.wait("vector", "sSY", mask_load)
            em.wait("tensor", "sVE", em.v("sVE"))
            attn_entry_pe = em.v("sPE")
            slot_free = [attn_entry_pe, attn_entry_pe]
            slot_loaded = [0, 0]
            norm_ve_prev = 0
            evac_prev = 0

            for p in range(NHP):
                sl = p % 2
                em.wait("gpsimd", "sPE", slot_free[sl])
                em.wait("gpsimd", "sSY", qk_spill[0][p])
                em.wait("gpsimd", "sSY", qk_spill[1][p])
                em.op("gpsimd", lambda e, p=p, sl=sl: e.dma_start(qkt[sl][0][:, :], qt_d[ts_(p), :]), "sGP", 16)
                em.op("gpsimd", lambda e, p=p, sl=sl: e.dma_start(qkt[sl][1][:, :], kt_d[ts_(p), :]), "sGP", 16)
                slot_loaded[sl] = em.v("sGP")
                em.wait("tensor", "sGP", slot_loaded[sl])
                if p == 0:
                    em.wait("tensor", "sAC", v_done)

                qg_seq = ([0] + list(range(NQG))) if p == 0 else list(range(NQG))
                for qg in qg_seq:
                    kcmax = min(4 * (qg + 1), NT) if is_a else NT
                    sac_base = em.v("sAC")
                    exp_marks = {}
                    pv_started = False

                    def emit_S(kc, qg=qg, sl=sl):
                        sps = (P0 if kc % 2 == 0 else P1).bitcast(F32)
                        if kc < 2:
                            em.wait("tensor", "sAC", sac_base)
                        else:
                            em.wait("tensor", "sAC", exp_marks[kc - 2])
                        masked = is_a and kc >= 4 * qg
                        em.op("tensor", lambda e, sps=sps, kc=kc, masked=masked: e.matmul(
                            sps[:, 0:512], qkt[sl][1][0:64, ts_(kc)], qkt[sl][0][0:64, ts_(qg, 512)],
                            start=True, stop=not masked, tile_position=(0, 0)), "sPE", None)
                        em.op("tensor", lambda e, sps=sps, kc=kc, masked=masked: e.matmul(
                            sps[:, 512:1024], qkt[sl][1][64:128, ts_(kc)], qkt[sl][0][64:128, ts_(qg, 512)],
                            start=True, stop=not masked, tile_position=(64, 0)), "sPE", 1 if not masked else None)
                        if masked:
                            mi = kc - 4 * qg
                            em.op("tensor", lambda e, sps=sps, mi=mi: e.matmul(
                                sps[:, 0:512], ident[:, :], mask_t[mi],
                                start=False, stop=True), "sPE", None)
                            em.op("tensor", lambda e, sps=sps, mi=mi: e.matmul(
                                sps[:, 512:1024], ident[:, :], mask_t[mi],
                                start=False, stop=True), "sPE", 1)
                        s_mark = em.v("sPE")
                        em.wait("scalar", "sPE", s_mark)
                        if is_a:
                            em.op("scalar", lambda e, sps=sps, kc=kc: e.activation(
                                p_sb[kc % 2][:, :], sps[:, :], AF.Exp, bias=nb_sh[:, :], scale=0.125),
                                "sAC", 1)
                        else:
                            em.op("scalar", lambda e, sps=sps, kc=kc: e.activation(
                                p_sb[kc % 2][:, :], sps[:, :], AF.Sigmoid, scale=0.125), "sAC", 1)
                        exp_marks[kc] = em.v("sAC")

                    def emit_PV(kc, qg=qg, p=p):
                        nonlocal pv_started, evac_prev
                        em.wait("tensor", "sAC", exp_marks[kc])
                        if not pv_started:
                            if evac_prev:
                                em.wait("tensor", "sAC" if is_a else "sVE", evac_prev)
                            if is_a and norm_ve_prev:
                                em.wait("tensor", "sVE", norm_ve_prev)
                        st = not pv_started
                        sp = (kc == kcmax - 1)
                        pv_started = True
                        em.op("tensor", lambda e, kc=kc, st=st, sp=sp, p=p: e.matmul(
                            P2.bitcast(F32)[0:64, 0:512],
                            vv[:, kc, (2 * p) * HD:(2 * p) * HD + HD],
                            p_sb[kc % 2][:, 0:512],
                            start=st, stop=sp, tile_position=(0, 0)), "sPE", None)
                        em.op("tensor", lambda e, kc=kc, st=st, sp=sp, p=p: e.matmul(
                            P2.bitcast(F32)[64:128, 512:1024],
                            vv[:, kc, (2 * p + 1) * HD:(2 * p + 1) * HD + HD],
                            p_sb[kc % 2][:, 512:1024],
                            start=st, stop=sp, tile_position=(0, 64)),
                            "sPE", None if is_a else (1 if sp else None))
                        if is_a:
                            em.op("tensor", lambda e, kc=kc, st=st, sp=sp: e.matmul(
                                P3.bitcast(F32)[0:1, 0:512], ones_col[:, :], p_sb[kc % 2][:, 0:512],
                                start=st, stop=sp, tile_position=(0, 0)), "sPE", None)
                            em.op("tensor", lambda e, kc=kc, st=st, sp=sp: e.matmul(
                                P3.bitcast(F32)[32:33, 512:1024], ones_col[:, :], p_sb[kc % 2][:, 512:1024],
                                start=st, stop=sp, tile_position=(0, 32)),
                                "sPE", 1 if sp else None)

                    for kc in range(kcmax):
                        emit_S(kc)
                        if kc >= 1:
                            emit_PV(kc - 1)
                    emit_PV(kcmax - 1)
                    pv_mark = em.v("sPE")
                    if is_a:
                        em.wait("scalar", "sPE", pv_mark)
                        em.op("scalar", lambda e: e.activation(
                            invzf[0:1, 0:512], P3.bitcast(F32)[0:1, 0:512], AF.Ln))
                        em.op("scalar", lambda e: e.activation(
                            invzf[32:33, 512:1024], P3.bitcast(F32)[32:33, 512:1024], AF.Ln), "sAC", 1)
                        em.wait("scalar", "sAC")
                        em.op("scalar", lambda e: e.activation(
                            invz[0:1, 0:512], invzf[0:1, 0:512], AF.Exp, scale=-1.0))
                        em.op("scalar", lambda e: e.activation(
                            invz[32:33, 512:1024], invzf[32:33, 512:1024], AF.Exp, scale=-1.0), "sAC", 1)
                        iv_mark = em.v("sAC")
                        if dbg == 6 and is_a and p == 0 and qg == 0:
                            em.wait("sync", "sVE", iv_mark)
                            em.op("sync", lambda e: e.dma_start(iz_dbg.rearrange("a n -> (a n)")[None, :], invzf[0:1, :]), "sSY", 16)
                            em.op("sync", lambda e: e.dma_start(p_dbg[:, 0:1024], p_sb[0][:, :]), "sSY", 16)
                            em.op("sync", lambda e: e.dma_start(p_dbg[:, 1024:2048], p_sb[1][:, :]), "sSY", 16)
                            em.wait("scalar", "sSY")
                            em.wait("vector", "sSY")
                            em.wait("tensor", "sSY")
                        em.wait("scalar", "sPE", pv_mark)
                        em.op("scalar", lambda e: e.copy(ot_tmp[0:64, 0:512], P2.bitcast(F32)[0:64, 0:512]))
                        em.op("scalar", lambda e: e.copy(ot_tmp[64:128, 0:512], P2.bitcast(F32)[64:128, 512:1024]),
                              "sAC", 1)
                        evac_prev = em.v("sAC")
                        em.wait("tensor", "sAC", iv_mark)
                        em.wait("tensor", "sAC", evac_prev)
                        em.op("tensor", lambda e: e.matmul(
                            P2.bitcast(F32)[0:64, 0:512], ones_bf[0:1, 0:64], invz[0:1, 0:512],
                            start=True, stop=True, tile_position=(0, 0)), "sPE", None)
                        em.op("tensor", lambda e: e.matmul(
                            P2.bitcast(F32)[64:128, 0:512], ones_bf[32:33, 0:64], invz[32:33, 512:1024],
                            start=True, stop=True, tile_position=(32, 64)), "sPE", 1)
                        bc_mark = em.v("sPE")
                        em.wait("vector", "sAC", evac_prev)
                        em.wait("vector", "sPE", bc_mark)
                        em.op("vector", lambda e, p=p, qg=qg: e.tensor_tensor(
                            otc[:, p, ts_(qg, 512)], ot_tmp[:, 0:512],
                            P2.bitcast(F32)[:, 0:512], op=ALU.mult), "sVE", 1)
                        norm_ve_prev = em.v("sVE")
                        if dbg == 5 and is_a and p == 0 and qg == 0:
                            em.op("vector", lambda e: e.tensor_copy(bc_dbg and None or None, None) if False else e.tensor_copy(
                                qt_stage[:, 0:1024].bitcast(F32)[:, 0:256], P0.bitcast(F32)[:, 0:256]), None)
                            em.prog["vector"].pop()
                            em.wait("sync", "sVE", norm_ve_prev)
                            em.op("sync", lambda e: e.dma_start(n_dbg[:, 0:512], otc[:, 0, 0:512]), "sSY", 16)
                            em.op("sync", lambda e: e.dma_start(n_dbg[:, 512:1024], ot_tmp[:, 0:512]), "sSY", 16)
                            em.wait("vector", "sVE", norm_ve_prev)
                            em.op("vector", lambda e: e.tensor_copy(invzf[:, 0:1024], P0.bitcast(F32)[:, 0:1024]), "sVE", 1)
                            em.wait("sync", "sVE")
                            em.op("sync", lambda e: e.dma_start(bc_dbg, invzf[:, 0:1024]), "sSY", 16)
                            em.wait("tensor", "sSY")
                            em.wait("vector", "sSY")
                            em.wait("scalar", "sSY")
                    else:
                        em.wait("vector", "sPE", pv_mark)
                        em.op("vector", lambda e, p=p, qg=qg: e.tensor_copy(
                            otc[0:64, p, ts_(qg, 512)], P2.bitcast(F32)[0:64, 0:512]))
                        em.op("vector", lambda e, p=p, qg=qg: e.tensor_copy(
                            otc[64:128, p, ts_(qg, 512)], P2.bitcast(F32)[64:128, 512:1024]), "sVE", 1)
                        evac_prev = em.v("sVE")
                slot_free[sl] = em.v("sPE")

            attn_ve = em.v("sVE")
            attn_ac = em.v("sAC")
            attn_pe = em.v("sPE")

            if dbg and is_a:
                em.wait("sync", "sVE", attn_ve)
                em.wait("sync", "sAC", attn_ac)
                for ccd in range(NCC):
                    em.op("sync", lambda e, ccd=ccd: e.dma_start(
                        ot_dbg[ts_(ccd)], otc[:, ccd, :]), "sSY", 16)
            xn = [B_view(k * 2048, 2048, F32) for k in range(2)]
            em.wait("sync", "sPE", attn_pe)
            em.wait("tensor", "sSY", wp_ld)
            pj_ev = [0, 0]
            st_mark = [0, 0]
            brow = 0 if is_a else 64
            first = True
            for i in range(NT):
                s = i % 2
                if st_mark[s]:
                    em.wait("sync", "sSY", st_mark[s])
                em.op("sync", lambda e, i=i, s=s: e.dma_start(xn[s], src[ts_(i), :]), "sSY", 16)
                ld = em.v("sSY")
                for ch in range(2):
                    pp = P1.bitcast(F32)[:, ch * 512:ch * 512 + 512]
                    if first:
                        em.wait("tensor", "sVE", attn_ve)
                        em.wait("tensor", "sAC", attn_ac)
                        first = False
                    if pj_ev[ch]:
                        em.wait("tensor", "sVE", pj_ev[ch])
                    for j in range(NCC):
                        em.op("tensor", lambda e, pp=pp, j=j, i=i, ch=ch: e.matmul(
                            pp, otc[:, j, ts_(i)], wA[:, j * C + ch * 512: j * C + ch * 512 + 512],
                            start=(j == 0), stop=False), "sPE", None)
                    em.op("tensor", lambda e, pp=pp, ch=ch, brow=brow: e.matmul(
                        pp, ones_bf[brow:brow + 1, :], biasrows[brow:brow + 1, ch * 512:ch * 512 + 512],
                        start=False, stop=True), "sPE", 1)
                    mmk = em.v("sPE")
                    em.wait("vector", "sPE", mmk)
                    em.wait("vector", "sSY", ld)
                    em.op("vector", lambda e, pp=pp, s=s, ch=ch: e.scalar_tensor_tensor(
                        xn[s][:, ch * 512:ch * 512 + 512], pp, 0.0,
                        xn[s][:, ch * 512:ch * 512 + 512], op0=ALU.add, op1=ALU.add), "sVE", 1)
                    pj_ev[ch] = em.v("sVE")
                em.wait("sync", "sVE", pj_ev[1])
                em.op("sync", lambda e, i=i, s=s: e.dma_start(dst[ts_(i), :], xn[s]), "sSY", 16)
                st_mark[s] = em.v("sSY")

        def emit_ffn(src, dst):
            for half in range(2):
                row0 = half * TH
                xlnTh = A_view(0, NCC * TH)
                base = NCC * TH
                xstg = [A_view(base + k * 2048, 2048, F32) for k in range(2)]
                lnstg = [A_view(base + 4096 + k * 2048, 2048, F32) for k in range(2)]
                h_sb = A_view(base + 8192, 2048)
                s_sb = [A_view(base + 10240 + k * 512, 512) for k in range(2)]
                y_full = B_view(0, 8 * T, F32)
                y_ap = y_full.rearrange("p (t c) -> p t c", c=C)[:, 0:NTH, :]

                emit_ln(src, row0, NTH, xlnTh, xstg, lnstg)
                xr = xlnTh.rearrange("p (c t) -> p c t", c=NCC)

                em.wait("tensor", "sVE")
                em.wait("tensor", "sAC")
                gate_tile_mark = {}
                sg_prev = 0
                chain_prev = 0
                for i in range(NTH):
                    pl = P3.bitcast(F32)[:, 0:E]
                    if sg_prev:
                        em.wait("tensor", "sAC", sg_prev)
                    for cc in range(NCC):
                        em.op("tensor", lambda e, pl=pl, cc=cc, i=i: e.matmul(
                            pl, xr[:, cc, ts_(i)], gk_sb[:, cc * E:(cc + 1) * E],
                            start=(cc == 0), stop=(cc == NCC - 1)), "sPE", 1 if cc == NCC - 1 else None)
                    lg = em.v("sPE")
                    em.wait("scalar", "sPE", lg)
                    if chain_prev:
                        em.wait("scalar", "sVE", chain_prev)
                    em.op("scalar", lambda e, pl=pl: e.activation(gsc[:, 0:E], pl, AF.Sigmoid), "sAC", 1)
                    sg_prev = em.v("sAC")
                    g = gsc
                    s4 = g[:, 0:E].rearrange("p (gr f) -> p gr f", gr=4)
                    em.wait("vector", "sAC", sg_prev)
                    em.vo(lambda e: e.tensor_tensor(g[:, 0:E], g[:, 0:E], gb_bc[:, :], op=ALU.add))
                    em.vo(lambda e: e.tensor_tensor(g[:, 20:24], s4[:, :, 0], s4[:, :, 1], op=ALU.max))
                    em.vo(lambda e: e.tensor_tensor(g[:, 24:28], s4[:, :, 0], s4[:, :, 1], op=ALU.min))
                    em.vo(lambda e: e.tensor_tensor(g[:, 28:32], s4[:, :, 2], s4[:, :, 3], op=ALU.max))
                    em.vo(lambda e: e.tensor_tensor(g[:, 32:36], s4[:, :, 2], s4[:, :, 3], op=ALU.min))
                    em.vo(lambda e: e.tensor_tensor(g[:, 36:40], g[:, 20:24], g[:, 28:32], op=ALU.max))
                    em.vo(lambda e: e.tensor_tensor(g[:, 40:44], g[:, 20:24], g[:, 28:32], op=ALU.min))
                    em.vo(lambda e: e.tensor_tensor(g[:, 44:48], g[:, 24:28], g[:, 32:36], op=ALU.max))
                    em.vo(lambda e: e.tensor_tensor(g[:, 44:48], g[:, 44:48], g[:, 40:44], op=ALU.max))
                    em.vo(lambda e: e.tensor_tensor(g[:, 16:20], g[:, 36:40], g[:, 44:48], op=ALU.add))
                    t2 = g[:, 16:20]
                    em.vo(lambda e: e.tensor_tensor(g[:, 49:50], t2[:, 0:1], t2[:, 1:2], op=ALU.max))
                    em.vo(lambda e: e.tensor_tensor(g[:, 50:51], t2[:, 0:1], t2[:, 1:2], op=ALU.min))
                    em.vo(lambda e: e.tensor_tensor(g[:, 51:52], t2[:, 2:3], t2[:, 3:4], op=ALU.max))
                    em.vo(lambda e: e.tensor_tensor(g[:, 52:53], t2[:, 2:3], t2[:, 3:4], op=ALU.min))
                    em.vo(lambda e: e.tensor_tensor(g[:, 53:54], g[:, 49:50], g[:, 51:52], op=ALU.min))
                    em.vo(lambda e: e.tensor_tensor(g[:, 48:49], g[:, 50:51], g[:, 52:53], op=ALU.max))
                    em.vo(lambda e: e.tensor_tensor(g[:, 48:49], g[:, 48:49], g[:, 53:54], op=ALU.max))
                    em.vo(lambda e: e.tensor_single_scalar(g[:, 54:58], t2, g[:, 48:49], op=ALU.is_ge))
                    for jj in range(4):
                        em.vo(lambda e, jj=jj: e.tensor_copy(
                            g[:, 64:80].rearrange("p (a b) -> p a b", a=4)[:, :, jj], g[:, 54:58]))
                    em.vo(lambda e: e.tensor_tensor(g[:, 80:96], g[:, 0:E], g[:, 64:80], op=ALU.mult))
                    em.vo(lambda e: e.reduce_max(g[:, 58:59], g[:, 80:96], axis=AX.X))
                    em.vo(lambda e: e.tensor_single_scalar(g[:, 112:128], g[:, 80:96], g[:, 58:59], op=ALU.is_ge))
                    em.vo(lambda e: e.scalar_tensor_tensor(
                        g[:, 96:112], g[:, 112:128], -1e30, g[:, 80:96], op0=ALU.mult, op1=ALU.add))
                    em.vo(lambda e: e.reduce_max(g[:, 58:59], g[:, 96:112], axis=AX.X))
                    for _ in range(2):
                        em.vo(lambda e: e.tensor_single_scalar(g[:, 112:128], g[:, 96:112], g[:, 58:59], op=ALU.is_ge))
                        em.vo(lambda e: e.scalar_tensor_tensor(
                            g[:, 96:112], g[:, 112:128], -1e30, g[:, 96:112], op0=ALU.mult, op1=ALU.add))
                        em.vo(lambda e: e.reduce_max(g[:, 58:59], g[:, 96:112], axis=AX.X))
                    em.vo(lambda e: e.tensor_single_scalar(
                        g[:, 96:112], g[:, 80:96], g[:, 58:59], op=ALU.is_ge))
                    em.vo(lambda e: e.tensor_tensor(g[:, 96:112], g[:, 96:112], g[:, 80:96], op=ALU.mult))
                    em.vo(lambda e: e.reduce_sum(g[:, 60:61], g[:, 96:112], axis=AX.X))
                    em.vo(lambda e: e.tensor_scalar_add(g[:, 60:61], g[:, 60:61], 1e-20))
                    em.vo(lambda e: e.reciprocal(g[:, 61:62], g[:, 60:61]))
                    em.vo(lambda e: e.tensor_scalar_mul(g[:, 61:62], g[:, 61:62], 2.5))
                    em.vo(lambda e, i=i: e.tensor_single_scalar(
                        comb[:, ts_(i, E)], g[:, 96:112], g[:, 61:62], op=ALU.mult))
                    chain_prev = em.v("sVE")
                    gate_tile_mark[i] = chain_prev
                gate_done = em.v("sVE")

                em.op("sync", lambda e, row0=row0: e.dma_start(
                    y_ap, src[row0:row0 + TH, :].rearrange("(t p) c -> p t c", p=128)), "sSY", 16)
                y_ld = em.v("sSY")
                em.wait("vector", "sSY", y_ld)

                wgb = [C_view(k * 12288, 4096) for k in range(2)]
                wub = [C_view(k * 12288 + 4096, 4096) for k in range(2)]
                wdb = [C_view(k * 12288 + 8192, 4096) for k in range(2)]
                wslot_free = [em.v("sPE"), em.v("sPE")]
                drot_mark = [0, 0, 0, 0]
                y_sl_mark = {}
                silu_mark = [0, 0]
                hmul_mark = [0, 0]

                def wsrcs(e_idx):
                    if e_idx < E:
                        return (wg[e_idx].rearrange("(cc p) m -> p cc m", p=128),
                                wu[e_idx].rearrange("(cc p) m -> p cc m", p=128),
                                wd[e_idx].rearrange("(mc p) c -> p mc c", p=128))
                    hh2 = e_idx - E
                    return (swg[:, hh2 * M:(hh2 + 1) * M].rearrange("(cc p) m -> p cc m", p=128),
                            swu[:, hh2 * M:(hh2 + 1) * M].rearrange("(cc p) m -> p cc m", p=128),
                            swd[hh2 * M:(hh2 + 1) * M, :].rearrange("(mc p) c -> p mc c", p=128))

                for ei in range(E + 2):
                    ws = ei % 2
                    em.wait("gpsimd", "sPE", wslot_free[ws])
                    sg_, su_, sd_ = wsrcs(ei)
                    em.op("gpsimd", lambda e, ws=ws, sg_=sg_: e.dma_start(
                        wgb[ws].rearrange("p (c m) -> p c m", c=NCC), sg_), "sGP", 16)
                    em.op("gpsimd", lambda e, ws=ws, su_=su_: e.dma_start(
                        wub[ws].rearrange("p (c m) -> p c m", c=NCC), su_), "sGP", 16)
                    em.op("gpsimd", lambda e, ws=ws, sd_=sd_: e.dma_start(
                        wdb[ws].rearrange("p (m c) -> p m c", m=4), sd_), "sGP", 16)
                    wload = em.v("sGP")
                    ce_col = one_col_f[:, :] if ei >= E else None
                    em.wait("tensor", "sGP", wload)
                    for sg in range(NSG):
                        if ei == 0:
                            em.wait("vector", "sVE", gate_tile_mark[sg * 4 + 3])
                        for m4 in range(4):
                            sl2 = m4 % 2
                            pg = P0.bitcast(F32)[:, sl2 * 512:sl2 * 512 + 512]
                            pu = P1.bitcast(F32)[:, sl2 * 512:sl2 * 512 + 512]
                            if silu_mark[sl2]:
                                em.wait("tensor", "sAC", silu_mark[sl2])
                            if hmul_mark[sl2]:
                                em.wait("tensor", "sVE", hmul_mark[sl2])
                            for cc in range(NCC):
                                em.op("tensor", lambda e, pg=pg, ws=ws, cc=cc, m4=m4, sg=sg: e.matmul(
                                    pg, wgb[ws].rearrange("p (c m) -> p c m", c=NCC)[:, cc, ts_(m4)],
                                    xr[:, cc, sg * 512:(sg + 1) * 512],
                                    start=(cc == 0), stop=(cc == NCC - 1)), "sPE", 1 if cc == NCC - 1 else None)
                            gmm = em.v("sPE")
                            em.wait("scalar", "sPE", gmm)
                            if hmul_mark[sl2]:
                                em.wait("scalar", "sVE", hmul_mark[sl2])
                            em.op("scalar", lambda e, pg=pg, sl2=sl2: e.activation(
                                s_sb[sl2][:, :], pg, AF.Silu), "sAC", 1)
                            silu_mark[sl2] = em.v("sAC")
                            for cc in range(NCC):
                                em.op("tensor", lambda e, pu=pu, ws=ws, cc=cc, m4=m4, sg=sg: e.matmul(
                                    pu, wub[ws].rearrange("p (c m) -> p c m", c=NCC)[:, cc, ts_(m4)],
                                    xr[:, cc, sg * 512:(sg + 1) * 512],
                                    start=(cc == 0), stop=(cc == NCC - 1)), "sPE", 1 if cc == NCC - 1 else None)
                            umm = em.v("sPE")
                            em.wait("vector", "sPE", umm)
                            em.wait("vector", "sAC", silu_mark[sl2])
                            em.op("vector", lambda e, pu=pu, m4=m4, sl2=sl2: e.tensor_tensor(
                                h_sb[:, ts_(m4, 512)], s_sb[sl2][:, :], pu, op=ALU.mult), "sVE", 1)
                            hmul_mark[sl2] = em.v("sVE")
                        h_done = em.v("sVE")
                        em.wait("tensor", "sVE", h_done)
                        for ti in range(4):
                            gtile = sg * 4 + ti
                            for ch in range(2):
                                dslot = (ti * 2 + ch) % 4
                                pd = (P2 if dslot < 2 else P3).bitcast(F32)[:, (dslot % 2) * 512:(dslot % 2) * 512 + 512]
                                if drot_mark[dslot]:
                                    em.wait("tensor", "sVE", drot_mark[dslot])
                                for m4 in range(4):
                                    em.op("tensor", lambda e, pd=pd, m4=m4, ti=ti, ws=ws, ch=ch: e.matmul(
                                        pd, h_sb[:, m4 * 512 + ti * 128: m4 * 512 + (ti + 1) * 128],
                                        wdb[ws].rearrange("p (m c) -> p m c", m=4)[:, m4, ch * 512:(ch + 1) * 512],
                                        start=(m4 == 0), stop=(m4 == 3)), "sPE", 1 if m4 == 3 else None)
                                dmm = em.v("sPE")
                                em.wait("vector", "sPE", dmm)
                                if (gtile, ch) in y_sl_mark:
                                    em.wait("vector", "sVE", y_sl_mark[(gtile, ch)])
                                sc_ap = ce_col if ce_col is not None else comb[:, gtile * E + ei: gtile * E + ei + 1]
                                em.op("vector", lambda e, pd=pd, gtile=gtile, ch=ch, sc_ap=sc_ap: e.scalar_tensor_tensor(
                                    y_ap[:, gtile, ch * 512:(ch + 1) * 512], pd, sc_ap,
                                    y_ap[:, gtile, ch * 512:(ch + 1) * 512],
                                    op0=ALU.mult, op1=ALU.add), "sVE", 1)
                                drot_mark[dslot] = em.v("sVE")
                                y_sl_mark[(gtile, ch)] = drot_mark[dslot]
                    wslot_free[ws] = em.v("sPE")
                y_done = em.v("sVE")
                em.wait("sync", "sVE", y_done)
                em.op("sync", lambda e, row0=row0: e.dma_start(
                    dst[row0:row0 + TH, :].rearrange("(t p) c -> p t c", p=128), y_ap), "sSY", 16)

        def prep_attn_b():
            em.wait("gpsimd", "sSY")
            em.wait("tensor", "sVE")
            em.wait("tensor", "sAC")
            stage = [B_view(0, 1024), B_view(1024, 1024)]
            tr_ps2 = [P2, P3]
            spill_mark = 0
            ev_done = {}
            idx = 0
            for (Wt, dstd) in [(bWq, wq_d), (bWk, wk_d), (bWv, wv_d), (bWo, wp_d)]:
                if spill_mark:
                    em.wait("vector", "sSY", spill_mark)
                for rr in range(NCC):
                    sl3 = idx % 2
                    if idx >= 2:
                        em.wait("gpsimd", "sPE", ev_done[(idx - 2, "tr")])
                    em.op("gpsimd", lambda e, Wt=Wt, rr=rr, sl3=sl3: e.dma_start(
                        stage[sl3][:, :], Wt[ts_(rr), :]), "sGP", 16)
                    gpm = em.v("sGP")
                    em.wait("tensor", "sGP", gpm)
                    if idx >= 2:
                        em.wait("tensor", "sVE", ev_done[(idx - 2, "ev")])
                    for cc in range(NCC):
                        em.op("tensor", lambda e, cc=cc, sl3=sl3: e.matmul(
                            tr_ps2[sl3].bitcast(BF16)[:, cc * 128:(cc + 1) * 128],
                            stage[sl3][:, ts_(cc)], ident[:, :],
                            start=(cc == 0), stop=(cc == NCC - 1), is_transpose=True),
                            "sPE", 1 if cc == NCC - 1 else None)
                    ev_done[(idx, "tr")] = em.v("sPE")
                    em.wait("vector", "sPE", ev_done[(idx, "tr")])
                    dst2 = W2[:, :].rearrange("p (c n) -> p c n", c=NCC)[:, :, rr * 128:(rr + 1) * 128]
                    em.op("vector", lambda e, dst2=dst2, sl3=sl3: e.tensor_copy(
                        dst2, tr_ps2[sl3].bitcast(BF16)[:, 0:1024].rearrange("p (c n) -> p c n", c=NCC)),
                        "sVE", 1)
                    ev_done[(idx, "ev")] = em.v("sVE")
                    idx += 1
                em.wait("sync", "sVE", em.v("sVE"))
                em.op("sync", lambda e, dstd=dstd: e.dma_start(
                    dstd.rearrange("(cc p) n -> p cc n", p=128),
                    W2[:, :].rearrange("p (c n) -> p c n", c=NCC)), "sSY", 16)
                spill_mark = em.v("sSY")

        emit_attn("a", x_in, y_out if n_sub == 1 else res_a)
        if n_sub >= 2:
            emit_ffn(res_a, y_out if n_sub == 2 else res_b)
        if n_sub >= 3:
            prep_attn_b()
            emit_attn("b", res_b, y_out if n_sub == 3 else res_a)
        if n_sub >= 4:
            emit_ffn(res_a, y_out)

        em.wait("sync", "sSY")
        em.wait("sync", "sGP")

        @block.sync
        def _(e):
            for f in em.prog["sync"]:
                f(e)

        @block.gpsimd
        def _(e):
            for f in em.prog["gpsimd"]:
                f(e)

        @block.tensor
        def _(e):
            for f in em.prog["tensor"]:
                f(e)

        @block.scalar
        def _(e):
            for f in em.prog["scalar"]:
                f(e)

        @block.vector
        def _(e):
            for f in em.prog["vector"]:
                f(e)

    return nc


def host_constants():
    ident = np.eye(128, dtype=ml_dtypes.bfloat16)
    ident_f = np.eye(128, dtype=np.float32)
    ones_bf = np.ones((128, 128), dtype=ml_dtypes.bfloat16)
    ones_f32 = np.ones((1, 128), dtype=np.float32)
    masks = np.zeros((4, 128, 512), dtype=ml_dtypes.bfloat16)
    for d in range(4):
        ki = np.arange(128)[:, None]
        qi = np.arange(512)[None, :]
        masks[d] = np.where(128 * d + ki <= qi, 0.0, -1e30).astype(ml_dtypes.bfloat16)
    return {"c_ident": ident, "c_ident_f": ident_f, "c_ones_bf": ones_bf,
            "c_ones_f32": ones_f32, "c_masks": masks}


def shard_inputs(inputs, n_cores=8):
    w = {
        "aWq": inputs["L0_attn_a_Wq"], "aWk": inputs["L0_attn_a_Wk"], "aWv": inputs["L0_attn_a_Wv"],
        "aAq": inputs["L0_attn_a_Aq"], "aAk": inputs["L0_attn_a_Ak"], "aAv": inputs["L0_attn_a_Av"],
        "aBq": inputs["L0_attn_a_Bq"], "aBk": inputs["L0_attn_a_Bk"], "aBv": inputs["L0_attn_a_Bv"],
        "aWp": inputs["L0_attn_a_Wp"], "abp": inputs["L0_attn_a_bp"],
        "aAp": inputs["L0_attn_a_Ap"], "aBp": inputs["L0_attn_a_Bp"],
        "bWq": inputs["L0_attn_b_Wq"], "bWk": inputs["L0_attn_b_Wk"],
        "bWv": inputs["L0_attn_b_Wv"], "bWo": inputs["L0_attn_b_Wo"],
        "bbq": inputs["L0_attn_b_bq"], "bbk": inputs["L0_attn_b_bk"],
        "bbv": inputs["L0_attn_b_bv"], "bbo": inputs["L0_attn_b_bo"],
        "bthq": inputs["L0_attn_b_th_q"], "bthk": inputs["L0_attn_b_th_k"],
        "gk": inputs["L0_ffn_gate_kernel"], "gb": inputs["L0_ffn_gate_bias"],
        "wg": inputs["L0_ffn_w_gate"], "wu": inputs["L0_ffn_w_up"], "wd": inputs["L0_ffn_w_down"],
        "swg": inputs["L0_ffn_sw_gate"], "swu": inputs["L0_ffn_sw_up"], "swd": inputs["L0_ffn_sw_down"],
    }
    w = {k: np.ascontiguousarray(np.asarray(v, dtype=np.float32)) for k, v in w.items()}
    consts = host_constants()
    x = np.ascontiguousarray(np.asarray(inputs["L0_attn_a_x"], dtype=np.float32))
    maps = []
    for c in range(n_cores):
        m = {"x": np.ascontiguousarray(x[c])}
        m.update(w)
        m.update(consts)
        maps.append(m)
    return maps


def kernel(**inputs):
    n_cores = 8
    maps = shard_inputs(inputs, n_cores)
    nc = build_core(4096)
    res = run_bass_kernel_spmd(nc, maps, core_ids=list(range(n_cores)))
    return np.stack([np.asarray(r["y"], dtype=np.float32) for r in res.results], axis=0)


# version 07 (final)
# speedup vs baseline: 1.0886x; 1.0488x over previous
import numpy as np
import ml_dtypes
from contextlib import ExitStack

import concourse.bass as bass
import concourse.mybir as mybir
from concourse.bass_utils import run_bass_kernel_spmd

F32 = mybir.dt.float32
BF16 = mybir.dt.bfloat16
AF = mybir.ActivationFunctionType
ALU = mybir.AluOpType
AX = mybir.AxisListType

C = 1024
H = 16
HD = 64
NCC = C // 128
NHP = H // 2
R = 8
SCALING = 16.0 / R
E = 16
M = 512
EPS = 1e-5
EXP_SHIFT = -12.0

ENGS = ("sync", "gpsimd", "tensor", "scalar", "vector")


class Em:
    def __init__(self):
        self.prog = {k: [] for k in ENGS}
        self.cnt = {}
        self.h = {}

    def add_sem(self, name, handle):
        self.h[name] = handle
        self.cnt[name] = 0

    def wait(self, eng, sem, val=None):
        if val is None:
            val = self.cnt[sem]
        assert val <= self.cnt[sem], f"wait on future count {sem} {val} > {self.cnt[sem]}"
        if val <= 0:
            return
        h = self.h[sem]
        self.prog[eng].append(lambda e, h=h, v=val: e.wait_ge(h, v))

    def op(self, eng, build, sem=None, n=1):
        if sem is None or n is None:
            self.prog[eng].append(build)
            return None
        h = self.h[sem]
        self.cnt[sem] += n
        self.prog[eng].append(lambda e, b=build, h=h, n=n: b(e).then_inc(h, n))
        return self.cnt[sem]

    def v(self, sem):
        return self.cnt[sem]

    def vo(self, build):
        self.wait("vector", "sVE")
        return self.op("vector", build, "sVE", 1)


def ts_(i, n=128):
    return slice(i * n, (i + 1) * n)


def build_core(T=4096, n_sub=4, dbg=False):
    NT = T // 128
    NQG = T // 512
    TH = T // 2
    NTH = TH // 128
    NSG = TH // 512

    nc = bass.Bass("TRN2", target_bir_lowering=False, detect_race_conditions=False)

    def din(name, shape, dtype=F32):
        return nc.dram_tensor(name, list(shape), dtype, kind="ExternalInput").ap()

    x_in = din("x", (T, C))
    aWq = din("aWq", (H, C, HD)); aWk = din("aWk", (H, C, HD)); aWv = din("aWv", (H, C, HD))
    aAq = din("aAq", (H, C, R)); aAk = din("aAk", (H, C, R)); aAv = din("aAv", (H, C, R))
    aBq = din("aBq", (H, R, HD)); aBk = din("aBk", (H, R, HD)); aBv = din("aBv", (H, R, HD))
    aWp = din("aWp", (C, C)); abp = din("abp", (C,))
    aAp = din("aAp", (C, R)); aBp = din("aBp", (R, C))
    bWq = din("bWq", (C, C)); bWk = din("bWk", (C, C)); bWv = din("bWv", (C, C)); bWo = din("bWo", (C, C))
    bbq = din("bbq", (C,)); bbk = din("bbk", (C,)); bbv = din("bbv", (C,)); bbo = din("bbo", (C,))
    bthq = din("bthq", (C,)); bthk = din("bthk", (C,))
    gkin = din("gk", (C, E)); gbin = din("gb", (E,))
    wg = din("wg", (E, C, M)); wu = din("wu", (E, C, M)); wd = din("wd", (E, M, C))
    swg = din("swg", (C, 2 * M)); swu = din("swu", (C, 2 * M)); swd = din("swd", (2 * M, C))
    c_ident = din("c_ident", (128, 128), BF16)
    c_ones_bf = din("c_ones_bf", (128, 128), BF16)
    c_ones_f32 = din("c_ones_f32", (1, 128), F32)
    c_masks = din("c_masks", (4, 128, 512), BF16)
    c_ident_f = din("c_ident_f", (128, 128), F32)

    y_out = nc.dram_tensor("y", [T, C], F32, kind="ExternalOutput").ap()
    res_a = nc.dram_tensor("res_a", [T, C], F32, kind="Internal").ap()
    res_b = nc.dram_tensor("res_b", [T, C], F32, kind="Internal").ap()
    kind_dbg = "ExternalOutput" if dbg else "Internal"
    qt_d = nc.dram_tensor("qt_d", [C, T], BF16, kind=kind_dbg).ap()
    kt_d = nc.dram_tensor("kt_d", [C, T], BF16, kind=kind_dbg).ap()
    wq_d = nc.dram_tensor("wq_d", [C, C], BF16, kind="Internal").ap()
    wk_d = nc.dram_tensor("wk_d", [C, C], BF16, kind="Internal").ap()
    wv_d = nc.dram_tensor("wv_d", [C, C], BF16, kind="Internal").ap()
    wp_d = nc.dram_tensor("wp_d", [C, C], BF16, kind="Internal").ap()
    xlt_dbg = nc.dram_tensor("xlt_dbg", [C, T], BF16, kind=kind_dbg).ap() if dbg else None
    ot_dbg = nc.dram_tensor("ot_dbg", [C, T], BF16, kind=kind_dbg).ap() if dbg else None
    iz_dbg = nc.dram_tensor("iz_dbg", [2, 512], F32, kind=kind_dbg).ap() if dbg else None
    p_dbg = nc.dram_tensor("p_dbg", [128, 4096], BF16, kind=kind_dbg).ap() if dbg else None
    n_dbg = nc.dram_tensor("n_dbg", [128, 1024], BF16, kind=kind_dbg).ap() if dbg else None
    bc_dbg = nc.dram_tensor("bc_dbg", [128, 1024], F32, kind=kind_dbg).ap() if dbg else None

    em = Em()

    with ExitStack() as cx:
        e_ = cx.enter_context

        A_cols = max(8 * T, 4 * T + 16384)
        A = e_(nc.sbuf_tensor("arenaA", [128, A_cols], BF16))
        B = e_(nc.sbuf_tensor("arenaB", [128, 8 * T], BF16))
        Ca = e_(nc.sbuf_tensor("arenaC", [128, 24576], BF16))
        W2 = e_(nc.sbuf_tensor("W2", [128, NCC * C], BF16))
        ident = e_(nc.sbuf_tensor("ident", [128, 128], BF16))
        ident_f = e_(nc.sbuf_tensor("ident_f", [128, 128], F32))
        ones_bf = e_(nc.sbuf_tensor("ones_bf", [128, 128], BF16))
        ones_f32 = e_(nc.sbuf_tensor("ones_f32", [1, 128], F32))
        ones_col = e_(nc.sbuf_tensor("ones_col", [128, 1], BF16))
        one_col_f = e_(nc.sbuf_tensor("one_col_f", [128, 1], F32))
        nb_sh = e_(nc.sbuf_tensor("nb_sh", [128, 1], F32))
        eps_t = e_(nc.sbuf_tensor("eps_t", [128, 1], F32))
        lnsc = e_(nc.sbuf_tensor("lnsc", [128, 24], F32))
        qt_stage = e_(nc.sbuf_tensor("qt_stage", [128, 4 * 512], BF16))
        junk_bf = qt_stage[:, 0:C]
        biasrows = e_(nc.sbuf_tensor("biasrows", [128, C], BF16))
        bqv = e_(nc.sbuf_tensor("bqv", [128, 8], F32))
        bkv = e_(nc.sbuf_tensor("bkv", [128, 8], F32))
        thqv = e_(nc.sbuf_tensor("thqv", [128, 8], F32))
        thkv = e_(nc.sbuf_tensor("thkv", [128, 8], F32))
        comb = e_(nc.sbuf_tensor("comb", [128, 16 * E], F32))
        gk_sb = e_(nc.sbuf_tensor("gk_sb", [128, NCC * E], BF16))
        gb4 = e_(nc.sbuf_tensor("gb4", [128, 4 * E], F32))
        gb4_row = e_(nc.sbuf_tensor("gb4_row", [1, 4 * E], F32))
        gsc = e_(nc.sbuf_tensor("gsc", [128, 256], F32))
        gsc4 = e_(nc.sbuf_tensor("gsc4", [128, 512], F32))

        P0 = e_(nc.psum_tensor("P0", [128, 1024], F32))
        P1 = e_(nc.psum_tensor("P1", [128, 1024], F32))
        P2 = e_(nc.psum_tensor("P2", [128, 1024], F32))
        P3 = e_(nc.psum_tensor("P3", [128, 1024], F32))

        for s in ("sSY", "sGP", "sPE", "sAC", "sVE"):
            em.add_sem(s, e_(nc.semaphore(s)))

        block = e_(nc.Block())

        def A_view(c0, n, dt=BF16):
            ap = A[:, c0:c0 + n]
            return ap.bitcast(dt) if dt is not BF16 else ap

        def B_view(c0, n, dt=BF16):
            ap = B[:, c0:c0 + n]
            return ap.bitcast(dt) if dt is not BF16 else ap

        def C_view(c0, n, dt=BF16):
            ap = Ca[:, c0:c0 + n]
            return ap.bitcast(dt) if dt is not BF16 else ap

        em.op("sync", lambda e: e.dma_start(ident[:, :], c_ident), "sSY", 16)
        em.op("sync", lambda e: e.dma_start(ident_f[:, :], c_ident_f), "sSY", 16)
        em.op("sync", lambda e: e.dma_start(ones_bf[:, :], c_ones_bf), "sSY", 16)
        em.op("sync", lambda e: e.dma_start(ones_f32[:, :], c_ones_f32), "sSY", 16)
        for j in range(4):
            em.op("sync", lambda e, j=j: e.dma_start(gb4_row[:, j * E:(j + 1) * E], gbin[None, :]), "sSY", 16)
        for j in range(NCC):
            em.op("sync", lambda e, j=j: e.dma_start(bqv[:, j:j + 1], bbq[ts_(j)][:, None]), "sSY", 16)
            em.op("sync", lambda e, j=j: e.dma_start(bkv[:, j:j + 1], bbk[ts_(j)][:, None]), "sSY", 16)
            em.op("sync", lambda e, j=j: e.dma_start(thqv[:, j:j + 1], bthq[ts_(j)][:, None]), "sSY", 16)
            em.op("sync", lambda e, j=j: e.dma_start(thkv[:, j:j + 1], bthk[ts_(j)][:, None]), "sSY", 16)
        em.op("gpsimd", lambda e: e.dma_start(biasrows[0:1, :], abp[None, :]), "sGP", 16)
        em.op("gpsimd", lambda e: e.dma_start(biasrows[32:33, :], bbv[None, :]), "sGP", 16)
        em.op("gpsimd", lambda e: e.dma_start(biasrows[64:65, :], bbo[None, :]), "sGP", 16)
        em.op("gpsimd", lambda e: e.dma_start(
            gk_sb[:, :].rearrange("p (c e) -> p c e", c=NCC),
            gkin.rearrange("(cc p) e -> p cc e", p=128)), "sGP", 16)
        em.op("vector", lambda e: e.memset(ones_col[:, :], 1.0))
        em.op("vector", lambda e: e.memset(one_col_f[:, :], 1.0))
        em.op("vector", lambda e: e.memset(nb_sh[:, :], EXP_SHIFT))
        em.op("vector", lambda e: e.memset(eps_t[:, :], EPS))
        em.wait("tensor", "sSY")
        em.op("tensor", lambda e: e.matmul(P3.bitcast(F32)[:, 0:4 * E], ones_f32[:, :], gb4_row[:, :],
                                           start=True, stop=True), "sPE", 1)
        em.wait("vector", "sPE")
        em.op("vector", lambda e: e.tensor_copy(gb4[:, :], P3.bitcast(F32)[:, 0:4 * E]), "sVE", 1)

        def fold_attn_a():
            a_stage = B_view(0, 2048, F32)
            bpr = Ca[0:8, 0:C]
            aqt = Ca[0:8, C:2 * C]
            ve_guard = 0
            spill_mark = 0
            for (Wt, At, Bt, dstd) in [(aWq, aAq, aBq, wq_d), (aWk, aAk, aBk, wk_d),
                                       (aWv, aAv, aBv, wv_d)]:
                dst = W2[:, :]
                if spill_mark:
                    em.wait("gpsimd", "sSY", spill_mark)
                for cc in range(NCC):
                    em.op("gpsimd", lambda e, Wt=Wt, dst=dst, cc=cc: e.dma_start(
                        dst[:, ts_(cc, C)].rearrange("p (h d) -> p h d", h=H),
                        Wt[:, ts_(cc), :].rearrange("h p d -> p h d")), "sGP", 16)
                if ve_guard:
                    em.wait("sync", "sVE", ve_guard)
                    em.wait("gpsimd", "sVE", ve_guard)
                em.op("sync", lambda e, At=At: e.dma_start(
                    a_stage.rearrange("p (h c r) -> p h c r", h=H, c=NCC),
                    At.rearrange("h (cc p) r -> p h cc r", p=128)), "sSY", 16)
                em.op("gpsimd", lambda e, Bt=Bt: e.dma_start(
                    bpr[:, :].rearrange("r (h d) -> r h d", h=H),
                    Bt.rearrange("h r d -> r h d")), "sGP", 16)
                base_gp = em.v("sGP")
                base_sy = em.v("sSY")
                for h in range(H):
                    em.wait("tensor", "sSY", base_sy)
                    if ve_guard:
                        em.wait("tensor", "sVE", ve_guard)
                    for cc in range(NCC):
                        src = a_stage.rearrange("p (h c r) -> p h c r", h=H, c=NCC)[:, h, cc, :]
                        em.op("tensor", lambda e, src=src, cc=cc: e.matmul(
                            P0.bitcast(F32)[0:8, ts_(cc)], src, ident_f[:, :],
                            start=(cc % 4 == 0), stop=(cc % 4 == 3), is_transpose=True),
                            "sPE", 1 if cc == NCC - 1 else None)
                    pe_tr = em.v("sPE")
                    em.wait("vector", "sPE", pe_tr)
                    em.op("vector", lambda e: e.tensor_scalar_mul(
                        aqt[0:8, :], P0.bitcast(F32)[0:8, :], SCALING), "sVE", 1)
                    ve_aqt = em.v("sVE")
                    em.wait("tensor", "sVE", ve_aqt)
                    em.wait("tensor", "sGP", base_gp)
                    for cc in range(NCC):
                        em.op("tensor", lambda e, cc=cc, h=h: e.matmul(
                            P1.bitcast(F32)[:, cc * HD:(cc + 1) * HD],
                            aqt[0:8, ts_(cc)], bpr[0:8, h * HD:(h + 1) * HD],
                            start=(cc == 0), stop=(cc == NCC - 1)), "sPE", 1 if cc == NCC - 1 else None)
                    pe_mm = em.v("sPE")
                    em.wait("vector", "sPE", pe_mm)
                    dstv = dst.rearrange("p (c hd) -> p c hd", c=NCC)
                    lora = P1.bitcast(F32)[:, 0:NCC * HD].rearrange("p (c hd) -> p c hd", c=NCC)
                    em.op("vector", lambda e, dstv=dstv, lora=lora, h=h: e.tensor_tensor(
                        dstv[:, :, h * HD:(h + 1) * HD], dstv[:, :, h * HD:(h + 1) * HD],
                        lora, op=ALU.add), "sVE", 1)
                    ve_guard = em.v("sVE")
                em.wait("sync", "sVE", ve_guard)
                em.op("sync", lambda e, dstd=dstd: e.dma_start(
                    dstd.rearrange("(cc p) n -> p cc n", p=128),
                    W2[:, :].rearrange("p (c n) -> p c n", c=NCC)), "sSY", 16)
                spill_mark = em.v("sSY")
            em.wait("gpsimd", "sSY", spill_mark)
            for j in range(NCC):
                em.op("gpsimd", lambda e, j=j: e.dma_start(W2[:, ts_(j, C)], aWp[ts_(j), :]), "sGP", 16)
            em.wait("sync", "sVE", ve_guard)
            em.wait("gpsimd", "sVE", ve_guard)
            em.op("sync", lambda e: e.dma_start(
                a_stage.rearrange("p (g r) -> p g r", g=128)[:, 0:NCC, :],
                aAp.rearrange("(cc p) r -> p cc r", p=128)), "sSY", 16)
            em.op("gpsimd", lambda e: e.dma_start(bpr[:, 0:C], aBp), "sGP", 16)
            sy2 = em.v("sSY")
            gp2 = em.v("sGP")
            em.wait("tensor", "sSY", sy2)
            em.wait("tensor", "sVE", ve_guard)
            for cc in range(NCC):
                src = a_stage.rearrange("p (g r) -> p g r", g=128)[:, cc, :]
                em.op("tensor", lambda e, src=src, cc=cc: e.matmul(
                    P0.bitcast(F32)[0:8, ts_(cc)], src, ident_f[:, :],
                    start=(cc % 4 == 0), stop=(cc % 4 == 3), is_transpose=True),
                    "sPE", 1 if cc == NCC - 1 else None)
            em.wait("vector", "sPE")
            em.op("vector", lambda e: e.tensor_scalar_mul(aqt[0:8, :], P0.bitcast(F32)[0:8, :], SCALING), "sVE", 1)
            em.wait("tensor", "sVE")
            em.wait("tensor", "sGP", gp2)
            add_prev = 0
            for cc in range(NCC):
                if add_prev:
                    em.wait("tensor", "sVE", add_prev)
                for nh in range(2):
                    em.op("tensor", lambda e, cc=cc, nh=nh: e.matmul(
                        P1.bitcast(F32)[:, ts_(nh, 512)],
                        aqt[0:8, ts_(cc)], bpr[0:8, ts_(nh, 512)],
                        start=True, stop=True), "sPE", 1 if nh == 1 else None)
                pe_mm = em.v("sPE")
                em.wait("vector", "sPE", pe_mm)
                em.op("vector", lambda e, cc=cc: e.tensor_tensor(
                    W2[:, ts_(cc, C)], W2[:, ts_(cc, C)], P1.bitcast(F32)[:, 0:C],
                    op=ALU.add), "sVE", 1)
                add_prev = em.v("sVE")

            em.wait("sync", "sVE", em.v("sVE"))
            em.op("sync", lambda e: e.dma_start(
                wp_d.rearrange("(cc p) n -> p cc n", p=128),
                W2[:, :].rearrange("p (c n) -> p c n", c=NCC)), "sSY", 16)

        fold_attn_a()

        def emit_ln(src, row0, n_tiles, xlnT, xstg, lnstg, guards=None, gp=None):
            ns = len(xstg)
            if gp:
                em.wait("sync", "sGP", gp)
            if guards is None:
                em.wait("sync", "sSY")
                em.wait("sync", "sPE")
            else:
                em.wait("sync", "sPE", guards["sPE"])
                em.wait("sync", "sVE", guards["sVE"])
            tr_mark = [0] * n_tiles
            ev_mark = [0] * n_tiles
            xl_mark = [0] * n_tiles
            for i in range(n_tiles):
                s = i % ns

                def lc(v, s=s):
                    return lnsc[:, v * 3 + s:v * 3 + s + 1]
                if i >= ns:
                    em.wait("sync", "sVE", xl_mark[i - ns])
                elif guards is not None:
                    em.wait("sync", "sVE", guards["sVE"])
                em.op("sync", lambda e, i=i, s=s: e.dma_start(
                    xstg[s], src[row0 + i * 128: row0 + (i + 1) * 128, :]), "sSY", 16)
                ld = em.v("sSY")
                em.wait("scalar", "sSY", ld)
                if i >= ns:
                    em.wait("scalar", "sVE", xl_mark[i - ns])
                elif guards is not None:
                    em.wait("scalar", "sVE", guards["sVE"])
                em.op("scalar", lambda e, s=s, lc=lc: e.activation(
                    junk_bf[:, :], xstg[s], AF.Square, accum_out=lc(1)), "sAC", 1)
                em.op("scalar", lambda e, s=s, lc=lc: e.activation(
                    junk_bf[:, :], xstg[s], AF.Copy, accum_out=lc(0)), "sAC", 1)
                sqm = em.v("sAC")
                em.wait("vector", "sAC", sqm)
                em.vo(lambda e, lc=lc: e.tensor_scalar_mul(lc(2), lc(0), 1.0 / C))
                em.vo(lambda e, lc=lc: e.tensor_tensor(lc(3), lc(2), lc(2), op=ALU.mult))
                em.vo(lambda e, lc=lc: e.scalar_tensor_tensor(
                    lc(4), lc(1), 1.0 / C, lc(3), op0=ALU.mult, op1=ALU.subtract))
                vm = em.v("sVE")
                em.wait("scalar", "sVE", vm)
                em.op("scalar", lambda e, lc=lc: e.activation(
                    lc(5), lc(4), AF.Ln, bias=eps_t[:, :]), "sAC", 1)
                em.op("scalar", lambda e, lc=lc: e.activation(
                    lc(6), lc(5), AF.Exp, scale=-0.5), "sAC", 1)
                sq2 = em.v("sAC")
                em.wait("vector", "sAC", sq2)
                if i >= 2:
                    em.wait("vector", "sPE", tr_mark[i - 2])
                i2 = i % 2
                em.vo(lambda e, s=s, i2=i2, lc=lc: e.tensor_scalar(
                    lnstg[i2], xstg[s], lc(2), lc(6),
                    op0=ALU.subtract, op1=ALU.mult))
                xl_mark[i] = em.v("sVE")
                ptr = P2 if i2 == 0 else P3
                em.wait("tensor", "sVE", xl_mark[i])
                if i >= 2:
                    em.wait("tensor", "sVE", ev_mark[i - 2])
                for cc in range(NCC):
                    em.op("tensor", lambda e, i2=i2, cc=cc, ptr=ptr: e.matmul(
                        ptr.bitcast(F32)[:, ts_(cc)], lnstg[i2][:, ts_(cc)], ident_f[:, :],
                        start=(cc % 4 == 0), stop=(cc % 4 == 3), is_transpose=True),
                        "sPE", 1 if cc == NCC - 1 else None)
                tr_mark[i] = em.v("sPE")
                em.wait("vector", "sPE", tr_mark[i])
                dstv = xlnT.rearrange("p (c t) -> p c t", c=NCC)[:, :, i * 128:(i + 1) * 128]
                em.op("vector", lambda e, dstv=dstv, ptr=ptr: e.tensor_copy(
                    dstv, ptr.bitcast(F32)[:, :].rearrange("p (c t) -> p c t", c=NCC)), "sVE", 1)
                ev_mark[i] = em.v("sVE")
            return {"ev": ev_mark, "tr_last": tr_mark[-1], "xl_last": xl_mark[-1]}

        def emit_attn(variant, src, dst, gp=None):
            is_a = variant == "a"
            xlnT = A_view(0, NCC * T)
            xstg = [B_view(k * 2048, 2048, F32) for k in range(2)]
            lnstg = [B_view(4096 + k * 2048, 2048, F32) for k in range(2)]
            emit_ln(src, 0, NT, xlnT, xstg, lnstg, gp=gp)
            xr = xlnT.rearrange("p (c t) -> p c t", c=NCC)
            if dbg == 2 and is_a:
                em.wait("sync", "sVE")
                for ccd in range(NCC):
                    em.op("sync", lambda e, ccd=ccd: e.dma_start(
                        xlt_dbg[ts_(ccd)], xr[:, ccd, :]), "sSY", 16)
                em.wait("tensor", "sSY")

            em.wait("sync", "sPE")
            if is_a:
                for mi in range(4):
                    em.op("sync", lambda e, mi=mi: e.dma_start(
                        C_view(18432 + mi * 512, 512), c_masks[mi]), "sSY", 16)
            wA = W2[:, :]
            wB = C_view(0, NCC * C)
            srcs = {"q": wq_d, "k": wk_d, "v": wv_d, "p": wp_d}
            em.op("sync", lambda e: e.dma_start(
                wA.rearrange("p (c n) -> p c n", c=NCC),
                srcs["q"].rearrange("(cc p) n -> p cc n", p=128)), "sSY", 16)
            em.op("sync", lambda e: e.dma_start(
                wB.rearrange("p (c n) -> p c n", c=NCC),
                srcs["k"].rearrange("(cc p) n -> p cc n", p=128)), "sSY", 16)
            wqk_ld = em.v("sSY")
            mask_load = em.v("sSY")
            qkv_entry_ve = em.v("sVE")
            qkv_entry_ac = em.v("sAC")
            em.wait("tensor", "sVE", qkv_entry_ve)
            em.wait("tensor", "sAC", qkv_entry_ac)
            em.wait("tensor", "sSY", wqk_ld)
            qs_mark = [0, 0, 0, 0]
            ev_rot = [0, 0, 0, 0]
            qk_spill = [[0] * NCC, [0] * NCC]
            qt_pe_done = 0
            rot = 0
            for which, (Wsb, dstd, bvec, thvec) in enumerate(
                    [(wA, qt_d, bqv, thqv), (wB, kt_d, bkv, thkv)]):
                if which == 1:
                    qt_pe_done = em.v("sPE")
                    em.wait("sync", "sPE", qt_pe_done)
                    em.op("sync", lambda e: e.dma_start(
                        wA.rearrange("p (c n) -> p c n", c=NCC),
                        srcs["v"].rearrange("(cc p) n -> p cc n", p=128)), "sSY", 16)
                for j in range(NCC):
                    for qg in range(NQG):
                        pq = (P0 if rot < 2 else P1).bitcast(F32)[:, (rot % 2) * 512:(rot % 2) * 512 + 512]
                        if ev_rot[rot]:
                            em.wait("tensor", "sAC" if is_a else "sVE", ev_rot[rot])
                        for cc in range(NCC):
                            em.op("tensor", lambda e, pq=pq, Wsb=Wsb, cc=cc, j=j, qg=qg: e.matmul(
                                pq, Wsb[:, cc * C + j * 128: cc * C + (j + 1) * 128],
                                xr[:, cc, qg * 512:(qg + 1) * 512],
                                start=(cc == 0), stop=(cc == NCC - 1)),
                                "sPE", 1 if cc == NCC - 1 else None)
                        mm_mark = em.v("sPE")
                        sslot = rot
                        if is_a:
                            em.wait("scalar", "sPE", mm_mark)
                            if qs_mark[sslot]:
                                em.wait("scalar", "sSY", qs_mark[sslot])
                            em.op("scalar", lambda e, pq=pq, sslot=sslot: e.copy(
                                qt_stage[:, ts_(sslot, 512)], pq), "sAC", 1)
                            ev_rot[rot] = em.v("sAC")
                            em.wait("sync", "sAC", ev_rot[rot])
                        else:
                            em.wait("vector", "sPE", mm_mark)
                            if qs_mark[sslot]:
                                em.wait("vector", "sSY", qs_mark[sslot])
                            em.op("vector", lambda e, pq=pq, sslot=sslot, bvec=bvec, thvec=thvec, j=j: e.tensor_scalar(
                                qt_stage[:, ts_(sslot, 512)], pq, bvec[:, j:j + 1], thvec[:, j:j + 1],
                                op0=ALU.add, op1=ALU.is_gt), "sVE", 1)
                            ev_rot[rot] = em.v("sVE")
                            em.wait("sync", "sVE", ev_rot[rot])
                        em.op("sync", lambda e, dstd=dstd, j=j, qg=qg, sslot=sslot: e.dma_start(
                            dstd[ts_(j), qg * 512:(qg + 1) * 512], qt_stage[:, ts_(sslot, 512)]), "sSY", 16)
                        qs_mark[sslot] = em.v("sSY")
                        rot = (rot + 1) % 4
                    qk_spill[which][j] = em.v("sSY")
            wv_ld = em.v("sSY")
            em.wait("tensor", "sSY", wv_ld)
            v_sb = B_view(0, 8 * T)
            vv = v_sb.rearrange("p (t hd) -> p t hd", t=NT)
            vrot_mark = [0, 0]
            for i in range(NT):
                for hh in range(2):
                    pv = (P2 if hh == 0 else P3).bitcast(F32)[:, 0:512]
                    if vrot_mark[hh]:
                        em.wait("tensor", "sAC", vrot_mark[hh])
                    for cc in range(NCC):
                        last = (cc == NCC - 1) and is_a
                        em.op("tensor", lambda e, pv=pv, cc=cc, i=i, hh=hh, last=last: e.matmul(
                            pv, xr[:, cc, i * 128:(i + 1) * 128],
                            wA[:, cc * C + hh * 512: cc * C + hh * 512 + 512],
                            start=(cc == 0), stop=last), "sPE", 1 if last else None)
                    if not is_a:
                        em.op("tensor", lambda e, pv=pv, hh=hh: e.matmul(
                            pv, ones_bf[32:33, :], biasrows[32:33, hh * 512:hh * 512 + 512],
                            start=False, stop=True), "sPE", 1)
                    mmv = em.v("sPE")
                    em.wait("scalar", "sPE", mmv)
                    em.op("scalar", lambda e, pv=pv, i=i, hh=hh: e.copy(
                        v_sb[:, i * C + hh * 512: i * C + hh * 512 + 512], pv), "sAC", 1)
                    vrot_mark[hh] = em.v("sAC")
            v_done = em.v("sAC")
            v_pe_done = em.v("sPE")
            em.wait("sync", "sPE", v_pe_done)
            em.op("sync", lambda e: e.dma_start(
                wA.rearrange("p (c n) -> p c n", c=NCC),
                srcs["p"].rearrange("(cc p) n -> p cc n", p=128)), "sSY", 16)
            wp_ld = em.v("sSY")

            otF = A_view(0, NCC * T)
            otc = otF.rearrange("p (k t) -> p k t", k=NHP)
            qkt = [(C_view(0, T), C_view(T, T)), (C_view(2 * T, T), C_view(3 * T, T))]
            p_sb = [C_view(4 * T + k * 1024, 1024) for k in range(2)]
            mask_t = [C_view(18432 + mi * 512, 512) for mi in range(4)]
            ot_tmp = C_view(20480, 1024)
            invz = C_view(21504, 1024)
            invzf = C_view(22528, 2048, F32)

            em.wait("scalar", "sVE", 5)
            em.wait("vector", "sSY", mask_load)
            em.wait("tensor", "sVE", em.v("sVE"))
            attn_entry_pe = em.v("sPE")
            slot_free = [attn_entry_pe, attn_entry_pe]
            slot_loaded = [0, 0]
            norm_ve_prev = 0
            evac_prev = 0

            for p in range(NHP):
                sl = p % 2
                em.wait("gpsimd", "sPE", slot_free[sl])
                em.wait("gpsimd", "sSY", qk_spill[0][p])
                em.wait("gpsimd", "sSY", qk_spill[1][p])
                em.op("gpsimd", lambda e, p=p, sl=sl: e.dma_start(qkt[sl][0][:, :], qt_d[ts_(p), :]), "sGP", 16)
                em.op("gpsimd", lambda e, p=p, sl=sl: e.dma_start(qkt[sl][1][:, :], kt_d[ts_(p), :]), "sGP", 16)
                slot_loaded[sl] = em.v("sGP")
                em.wait("tensor", "sGP", slot_loaded[sl])
                if p == 0:
                    em.wait("tensor", "sAC", v_done)

                qg_seq = ([0] + list(range(NQG))) if p == 0 else list(range(NQG))
                for qg in qg_seq:
                    kcmax = min(4 * (qg + 1), NT) if is_a else NT
                    sac_base = em.v("sAC")
                    exp_marks = {}
                    pv_started = False

                    def emit_S(kc, qg=qg, sl=sl):
                        sps = (P0 if kc % 2 == 0 else P1).bitcast(F32)
                        if kc < 2:
                            em.wait("tensor", "sAC", sac_base)
                        else:
                            em.wait("tensor", "sAC", exp_marks[kc - 2])
                        masked = is_a and kc >= 4 * qg
                        em.op("tensor", lambda e, sps=sps, kc=kc, masked=masked: e.matmul(
                            sps[:, 0:512], qkt[sl][1][0:64, ts_(kc)], qkt[sl][0][0:64, ts_(qg, 512)],
                            start=True, stop=not masked, tile_position=(0, 0)), "sPE", None)
                        em.op("tensor", lambda e, sps=sps, kc=kc, masked=masked: e.matmul(
                            sps[:, 512:1024], qkt[sl][1][64:128, ts_(kc)], qkt[sl][0][64:128, ts_(qg, 512)],
                            start=True, stop=not masked, tile_position=(64, 0)), "sPE", 1 if not masked else None)
                        if masked:
                            mi = kc - 4 * qg
                            em.op("tensor", lambda e, sps=sps, mi=mi: e.matmul(
                                sps[:, 0:512], ident[:, :], mask_t[mi],
                                start=False, stop=True), "sPE", None)
                            em.op("tensor", lambda e, sps=sps, mi=mi: e.matmul(
                                sps[:, 512:1024], ident[:, :], mask_t[mi],
                                start=False, stop=True), "sPE", 1)
                        s_mark = em.v("sPE")
                        em.wait("scalar", "sPE", s_mark)
                        if is_a:
                            em.op("scalar", lambda e, sps=sps, kc=kc: e.activation(
                                p_sb[kc % 2][:, :], sps[:, :], AF.Exp, bias=nb_sh[:, :], scale=0.125),
                                "sAC", 1)
                        else:
                            em.op("scalar", lambda e, sps=sps, kc=kc: e.activation(
                                p_sb[kc % 2][:, :], sps[:, :], AF.Sigmoid, scale=0.125), "sAC", 1)
                        exp_marks[kc] = em.v("sAC")

                    def emit_PV(kc, qg=qg, p=p):
                        nonlocal pv_started, evac_prev
                        em.wait("tensor", "sAC", exp_marks[kc])
                        if not pv_started:
                            if evac_prev:
                                em.wait("tensor", "sAC" if is_a else "sVE", evac_prev)
                            if is_a and norm_ve_prev:
                                em.wait("tensor", "sVE", norm_ve_prev)
                        st = not pv_started
                        sp = (kc == kcmax - 1)
                        pv_started = True
                        em.op("tensor", lambda e, kc=kc, st=st, sp=sp, p=p: e.matmul(
                            P2.bitcast(F32)[0:64, 0:512],
                            vv[:, kc, (2 * p) * HD:(2 * p) * HD + HD],
                            p_sb[kc % 2][:, 0:512],
                            start=st, stop=sp, tile_position=(0, 0)), "sPE", None)
                        em.op("tensor", lambda e, kc=kc, st=st, sp=sp, p=p: e.matmul(
                            P2.bitcast(F32)[64:128, 512:1024],
                            vv[:, kc, (2 * p + 1) * HD:(2 * p + 1) * HD + HD],
                            p_sb[kc % 2][:, 512:1024],
                            start=st, stop=sp, tile_position=(0, 64)),
                            "sPE", None if is_a else (1 if sp else None))
                        if is_a:
                            em.op("tensor", lambda e, kc=kc, st=st, sp=sp: e.matmul(
                                P3.bitcast(F32)[0:1, 0:512], ones_col[:, :], p_sb[kc % 2][:, 0:512],
                                start=st, stop=sp, tile_position=(0, 0)), "sPE", None)
                            em.op("tensor", lambda e, kc=kc, st=st, sp=sp: e.matmul(
                                P3.bitcast(F32)[32:33, 512:1024], ones_col[:, :], p_sb[kc % 2][:, 512:1024],
                                start=st, stop=sp, tile_position=(0, 32)),
                                "sPE", 1 if sp else None)

                    for kc in range(kcmax):
                        emit_S(kc)
                        if kc >= 1:
                            emit_PV(kc - 1)
                    emit_PV(kcmax - 1)
                    pv_mark = em.v("sPE")
                    if is_a:
                        em.wait("scalar", "sPE", pv_mark)
                        em.op("scalar", lambda e: e.activation(
                            invzf[0:1, 0:512], P3.bitcast(F32)[0:1, 0:512], AF.Ln))
                        em.op("scalar", lambda e: e.activation(
                            invzf[32:33, 512:1024], P3.bitcast(F32)[32:33, 512:1024], AF.Ln), "sAC", 1)
                        em.wait("scalar", "sAC")
                        em.op("scalar", lambda e: e.activation(
                            invz[0:1, 0:512], invzf[0:1, 0:512], AF.Exp, scale=-1.0))
                        em.op("scalar", lambda e: e.activation(
                            invz[32:33, 512:1024], invzf[32:33, 512:1024], AF.Exp, scale=-1.0), "sAC", 1)
                        iv_mark = em.v("sAC")
                        if dbg == 6 and is_a and p == 0 and qg == 0:
                            em.wait("sync", "sVE", iv_mark)
                            em.op("sync", lambda e: e.dma_start(iz_dbg.rearrange("a n -> (a n)")[None, :], invzf[0:1, :]), "sSY", 16)
                            em.op("sync", lambda e: e.dma_start(p_dbg[:, 0:1024], p_sb[0][:, :]), "sSY", 16)
                            em.op("sync", lambda e: e.dma_start(p_dbg[:, 1024:2048], p_sb[1][:, :]), "sSY", 16)
                            em.wait("scalar", "sSY")
                            em.wait("vector", "sSY")
                            em.wait("tensor", "sSY")
                        em.wait("scalar", "sPE", pv_mark)
                        em.op("scalar", lambda e: e.copy(ot_tmp[0:64, 0:512], P2.bitcast(F32)[0:64, 0:512]))
                        em.op("scalar", lambda e: e.copy(ot_tmp[64:128, 0:512], P2.bitcast(F32)[64:128, 512:1024]),
                              "sAC", 1)
                        evac_prev = em.v("sAC")
                        em.wait("tensor", "sAC", iv_mark)
                        em.wait("tensor", "sAC", evac_prev)
                        em.op("tensor", lambda e: e.matmul(
                            P2.bitcast(F32)[0:64, 0:512], ones_bf[0:1, 0:64], invz[0:1, 0:512],
                            start=True, stop=True, tile_position=(0, 0)), "sPE", None)
                        em.op("tensor", lambda e: e.matmul(
                            P2.bitcast(F32)[64:128, 0:512], ones_bf[32:33, 0:64], invz[32:33, 512:1024],
                            start=True, stop=True, tile_position=(32, 64)), "sPE", 1)
                        bc_mark = em.v("sPE")
                        em.wait("vector", "sAC", evac_prev)
                        em.wait("vector", "sPE", bc_mark)
                        em.op("vector", lambda e, p=p, qg=qg: e.tensor_tensor(
                            otc[:, p, ts_(qg, 512)], ot_tmp[:, 0:512],
                            P2.bitcast(F32)[:, 0:512], op=ALU.mult), "sVE", 1)
                        norm_ve_prev = em.v("sVE")
                        if dbg == 5 and is_a and p == 0 and qg == 0:
                            em.op("vector", lambda e: e.tensor_copy(bc_dbg and None or None, None) if False else e.tensor_copy(
                                qt_stage[:, 0:1024].bitcast(F32)[:, 0:256], P0.bitcast(F32)[:, 0:256]), None)
                            em.prog["vector"].pop()
                            em.wait("sync", "sVE", norm_ve_prev)
                            em.op("sync", lambda e: e.dma_start(n_dbg[:, 0:512], otc[:, 0, 0:512]), "sSY", 16)
                            em.op("sync", lambda e: e.dma_start(n_dbg[:, 512:1024], ot_tmp[:, 0:512]), "sSY", 16)
                            em.wait("vector", "sVE", norm_ve_prev)
                            em.op("vector", lambda e: e.tensor_copy(invzf[:, 0:1024], P0.bitcast(F32)[:, 0:1024]), "sVE", 1)
                            em.wait("sync", "sVE")
                            em.op("sync", lambda e: e.dma_start(bc_dbg, invzf[:, 0:1024]), "sSY", 16)
                            em.wait("tensor", "sSY")
                            em.wait("vector", "sSY")
                            em.wait("scalar", "sSY")
                    else:
                        em.wait("vector", "sPE", pv_mark)
                        em.op("vector", lambda e, p=p, qg=qg: e.tensor_copy(
                            otc[0:64, p, ts_(qg, 512)], P2.bitcast(F32)[0:64, 0:512]))
                        em.op("vector", lambda e, p=p, qg=qg: e.tensor_copy(
                            otc[64:128, p, ts_(qg, 512)], P2.bitcast(F32)[64:128, 512:1024]), "sVE", 1)
                        evac_prev = em.v("sVE")
                slot_free[sl] = em.v("sPE")

            attn_ve = em.v("sVE")
            attn_ac = em.v("sAC")
            attn_pe = em.v("sPE")

            if dbg and is_a:
                em.wait("sync", "sVE", attn_ve)
                em.wait("sync", "sAC", attn_ac)
                for ccd in range(NCC):
                    em.op("sync", lambda e, ccd=ccd: e.dma_start(
                        ot_dbg[ts_(ccd)], otc[:, ccd, :]), "sSY", 16)
            xn = [B_view(k * 2048, 2048, F32) for k in range(2)]
            em.wait("sync", "sPE", attn_pe)
            em.wait("tensor", "sSY", wp_ld)
            pj_ev = [0, 0]
            st_mark = [0, 0]
            brow = 0 if is_a else 64
            first = True
            for i in range(NT):
                s = i % 2
                if st_mark[s]:
                    em.wait("sync", "sSY", st_mark[s])
                em.op("sync", lambda e, i=i, s=s: e.dma_start(xn[s], src[ts_(i), :]), "sSY", 16)
                ld = em.v("sSY")
                for ch in range(2):
                    pp = P1.bitcast(F32)[:, ch * 512:ch * 512 + 512]
                    if first:
                        em.wait("tensor", "sVE", attn_ve)
                        em.wait("tensor", "sAC", attn_ac)
                        first = False
                    if pj_ev[ch]:
                        em.wait("tensor", "sVE", pj_ev[ch])
                    for j in range(NCC):
                        em.op("tensor", lambda e, pp=pp, j=j, i=i, ch=ch: e.matmul(
                            pp, otc[:, j, ts_(i)], wA[:, j * C + ch * 512: j * C + ch * 512 + 512],
                            start=(j == 0), stop=False), "sPE", None)
                    em.op("tensor", lambda e, pp=pp, ch=ch, brow=brow: e.matmul(
                        pp, ones_bf[brow:brow + 1, :], biasrows[brow:brow + 1, ch * 512:ch * 512 + 512],
                        start=False, stop=True), "sPE", 1)
                    mmk = em.v("sPE")
                    em.wait("vector", "sPE", mmk)
                    em.wait("vector", "sSY", ld)
                    em.op("vector", lambda e, pp=pp, s=s, ch=ch: e.scalar_tensor_tensor(
                        xn[s][:, ch * 512:ch * 512 + 512], pp, 0.0,
                        xn[s][:, ch * 512:ch * 512 + 512], op0=ALU.add, op1=ALU.add), "sVE", 1)
                    pj_ev[ch] = em.v("sVE")
                em.wait("sync", "sVE", pj_ev[1])
                em.op("sync", lambda e, i=i, s=s: e.dma_start(dst[ts_(i), :], xn[s]), "sSY", 16)
                st_mark[s] = em.v("sSY")

        def emit_ffn(src, dst):
            xlnTh = A_view(0, NCC * TH)
            base = NCC * TH
            xstg = [A_view(base + k * 2048, 2048, F32) for k in range(3)]
            lnstg = [A_view(base + 6144 + k * 2048, 2048, F32) for k in range(2)]
            h_sb = A_view(base + 10240, 2048)
            s_sb = [A_view(base + 12288 + k * 512, 512) for k in range(2)]
            y_full = B_view(0, 8 * T, F32)
            y_ap = y_full.rearrange("p (t c) -> p t c", c=C)[:, 0:NTH, :]
            xr = xlnTh.rearrange("p (c t) -> p c t", c=NCC)

            wgb = [C_view(k * 12288, 4096) for k in range(2)]
            wub = [C_view(k * 12288 + 4096, 4096) for k in range(2)]
            wdb = [C_view(k * 12288 + 8192, 4096) for k in range(2)]
            wslot_free = [em.v("sPE"), em.v("sPE")]
            drot_mark = [0, 0, 0, 0]
            silu_mark = [0, 0]
            hmul_mark = [0, 0]

            def wsrcs(e_idx):
                if e_idx < E:
                    return (wg[e_idx].rearrange("(cc p) m -> p cc m", p=128),
                            wu[e_idx].rearrange("(cc p) m -> p cc m", p=128),
                            wd[e_idx].rearrange("(mc p) c -> p mc c", p=128))
                hh2 = e_idx - E
                return (swg[:, hh2 * M:(hh2 + 1) * M].rearrange("(cc p) m -> p cc m", p=128),
                        swu[:, hh2 * M:(hh2 + 1) * M].rearrange("(cc p) m -> p cc m", p=128),
                        swd[hh2 * M:(hh2 + 1) * M, :].rearrange("(mc p) c -> p mc c", p=128))

            def emit_load(ei, ws):
                em.wait("gpsimd", "sPE", wslot_free[ws])
                sg_, su_, sd_ = wsrcs(ei)
                em.op("gpsimd", lambda e, ws=ws, sg_=sg_: e.dma_start(
                    wgb[ws].rearrange("p (c m) -> p c m", c=NCC), sg_), "sGP", 16)
                em.op("gpsimd", lambda e, ws=ws, su_=su_: e.dma_start(
                    wub[ws].rearrange("p (c m) -> p c m", c=NCC), su_), "sGP", 16)
                em.op("gpsimd", lambda e, ws=ws, sd_=sd_: e.dma_start(
                    wdb[ws].rearrange("p (m c) -> p m c", m=4), sd_), "sGP", 16)
                return em.v("sGP")

            prev_guards = None
            wb_mark = 0
            for half in range(2):
                row0 = half * TH
                loads_mark = {E: emit_load(E, 0), E + 1: emit_load(E + 1, 1)}
                em.wait("tensor", "sVE")
                em.wait("tensor", "sAC")
                marks = emit_ln(src, row0, NTH, xlnTh, xstg, lnstg, guards=prev_guards)
                if wb_mark:
                    em.wait("sync", "sGP", wb_mark)
                em.op("sync", lambda e, row0=row0: e.dma_start(
                    y_ap, src[row0:row0 + TH, :].rearrange("(t p) c -> p t c", p=128)), "sSY", 16)
                y_ld = em.v("sSY")

                exp_mark = [0] * NSG
                for sg in range(NSG):
                    pb = (P0 if sg % 2 == 0 else P1).bitcast(F32)[:, (sg // 2) * 64:(sg // 2) * 64 + 64]
                    em.wait("tensor", "sVE", marks["ev"][sg * 4 + 3])
                    if sg >= 2:
                        em.wait("tensor", "sAC", exp_mark[sg - 2])
                    for j in range(4):
                        for cc in range(NCC):
                            em.op("tensor", lambda e, pb=pb, cc=cc, j=j, sg=sg: e.matmul(
                                pb[:, j * E:(j + 1) * E], xr[:, cc, ts_(sg * 4 + j)],
                                gk_sb[:, cc * E:(cc + 1) * E],
                                start=(j == 0 and cc == 0), stop=(j == 3 and cc == NCC - 1)),
                                "sPE", 1 if (j == 3 and cc == NCC - 1) else None)
                    lgm = em.v("sPE")
                    em.wait("scalar", "sPE", lgm)
                    em.op("scalar", lambda e, pb=pb, sg=sg: e.activation(
                        gsc4[:, 256 + sg * 64:256 + sg * 64 + 64], pb, AF.Exp, scale=-1.0), "sAC", 1)
                    exp_mark[sg] = em.v("sAC")

                def chains(sg):
                    gw = gsc
                    sgc = gsc4[:, sg * 64:sg * 64 + 64]
                    tq = gsc4[:, 256 + sg * 64:256 + sg * 64 + 64]

                    def h16(c0):
                        return gw[:, c0:c0 + 16].rearrange("p (j g) -> p j g", j=4)
                    em.wait("vector", "sAC", exp_mark[sg])
                    em.vo(lambda e: e.tensor_scalar_add(tq, tq, 1.0))
                    em.vo(lambda e: e.reciprocal(sgc, tq))
                    em.vo(lambda e: e.tensor_tensor(gw[:, 0:64], sgc, gb4[:, :], op=ALU.add))
                    sb4 = gw[:, 0:64].rearrange("p (j g f) -> p j g f", j=4, g=4)
                    em.vo(lambda e: e.tensor_tensor(h16(64), sb4[:, :, :, 0], sb4[:, :, :, 1], op=ALU.max))
                    em.vo(lambda e: e.tensor_tensor(h16(80), sb4[:, :, :, 0], sb4[:, :, :, 1], op=ALU.min))
                    em.vo(lambda e: e.tensor_tensor(h16(96), sb4[:, :, :, 2], sb4[:, :, :, 3], op=ALU.max))
                    em.vo(lambda e: e.tensor_tensor(h16(112), sb4[:, :, :, 2], sb4[:, :, :, 3], op=ALU.min))
                    em.vo(lambda e: e.tensor_tensor(gw[:, 128:144], gw[:, 64:80], gw[:, 96:112], op=ALU.max))
                    em.vo(lambda e: e.tensor_tensor(gw[:, 144:160], gw[:, 64:80], gw[:, 96:112], op=ALU.min))
                    em.vo(lambda e: e.tensor_tensor(gw[:, 160:176], gw[:, 80:96], gw[:, 112:128], op=ALU.max))
                    em.vo(lambda e: e.tensor_tensor(gw[:, 160:176], gw[:, 160:176], gw[:, 144:160], op=ALU.max))
                    em.vo(lambda e: e.tensor_tensor(gw[:, 176:192], gw[:, 128:144], gw[:, 160:176], op=ALU.add))
                    t2v = h16(176)
                    em.vo(lambda e: e.tensor_tensor(gw[:, 192:196], t2v[:, :, 0], t2v[:, :, 1], op=ALU.max))
                    em.vo(lambda e: e.tensor_tensor(gw[:, 196:200], t2v[:, :, 0], t2v[:, :, 1], op=ALU.min))
                    em.vo(lambda e: e.tensor_tensor(gw[:, 200:204], t2v[:, :, 2], t2v[:, :, 3], op=ALU.max))
                    em.vo(lambda e: e.tensor_tensor(gw[:, 204:208], t2v[:, :, 2], t2v[:, :, 3], op=ALU.min))
                    em.vo(lambda e: e.tensor_tensor(gw[:, 208:212], gw[:, 192:196], gw[:, 200:204], op=ALU.min))
                    em.vo(lambda e: e.tensor_tensor(gw[:, 212:216], gw[:, 196:200], gw[:, 204:208], op=ALU.max))
                    em.vo(lambda e: e.tensor_tensor(gw[:, 208:212], gw[:, 208:212], gw[:, 212:216], op=ALU.max))
                    for j in range(4):
                        em.vo(lambda e, j=j: e.tensor_single_scalar(
                            gw[:, 216 + j * 4:220 + j * 4], t2v[:, j, :], gw[:, 208 + j:209 + j], op=ALU.is_ge))
                    sm4 = gw[:, 64:128].rearrange("p (j g f) -> p j g f", j=4, g=4)
                    gmv = h16(216)
                    for f in range(4):
                        em.vo(lambda e, f=f: e.tensor_copy(sm4[:, :, :, f], gmv))
                    em.vo(lambda e: e.scalar_tensor_tensor(
                        gw[:, 0:64], gw[:, 64:128], 1.0, gw[:, 0:64], op0=ALU.mult, op1=ALU.mult))

                    def rtree(c0, op):
                        def pair(a, w):
                            return gw[:, a:a + w].rearrange("p (j q r) -> p j q r", j=4, r=2)
                        em.vo(lambda e: e.tensor_tensor(
                            gw[:, 128:160].rearrange("p (j q) -> p j q", j=4),
                            pair(c0, 64)[:, :, :, 0], pair(c0, 64)[:, :, :, 1], op=op))
                        em.vo(lambda e: e.tensor_tensor(
                            gw[:, 160:176].rearrange("p (j q) -> p j q", j=4),
                            pair(128, 32)[:, :, :, 0], pair(128, 32)[:, :, :, 1], op=op))
                        em.vo(lambda e: e.tensor_tensor(
                            gw[:, 176:184].rearrange("p (j q) -> p j q", j=4),
                            pair(160, 16)[:, :, :, 0], pair(160, 16)[:, :, :, 1], op=op))
                        em.vo(lambda e: e.tensor_tensor(
                            gw[:, 184:188],
                            gw[:, 176:184].rearrange("p (j r) -> p j r", j=4)[:, :, 0],
                            gw[:, 176:184].rearrange("p (j r) -> p j r", j=4)[:, :, 1], op=op))

                    for rnd in range(3):
                        src_c = 0 if rnd == 0 else 64
                        rtree(src_c, ALU.max)
                        for j in range(4):
                            em.vo(lambda e, j=j, src_c=src_c: e.tensor_single_scalar(
                                gw[:, 192 + j * 16:192 + j * 16 + 16],
                                gw[:, src_c + j * 16:src_c + j * 16 + 16],
                                gw[:, 184 + j:185 + j], op=ALU.is_ge))
                        em.vo(lambda e, src_c=src_c: e.scalar_tensor_tensor(
                            gw[:, 64:128], gw[:, 192:256], -1e30, gw[:, src_c:src_c + 64],
                            op0=ALU.mult, op1=ALU.add))
                    rtree(64, ALU.max)
                    for j in range(4):
                        em.vo(lambda e, j=j: e.tensor_single_scalar(
                            gw[:, 192 + j * 16:192 + j * 16 + 16], gw[:, j * 16:j * 16 + 16],
                            gw[:, 184 + j:185 + j], op=ALU.is_ge))
                    em.vo(lambda e: e.tensor_tensor(gw[:, 64:128], gw[:, 192:256], gw[:, 0:64], op=ALU.mult))
                    rtree(64, ALU.add)
                    em.vo(lambda e: e.tensor_scalar_add(gw[:, 184:188], gw[:, 184:188], 1e-20))
                    em.vo(lambda e: e.reciprocal(gw[:, 188:192], gw[:, 184:188]))
                    em.vo(lambda e: e.tensor_scalar_mul(gw[:, 188:192], gw[:, 188:192], 2.5))
                    for j in range(4):
                        em.vo(lambda e, j=j, sg=sg: e.tensor_single_scalar(
                            comb[:, (sg * 4 + j) * E:(sg * 4 + j + 1) * E],
                            gw[:, 64 + j * 16:64 + j * 16 + 16],
                            gw[:, 188 + j:189 + j], op=ALU.mult))

                first_stt = [True]

                def sg_body(ei, ws, sg, wl):
                    if sg == 0:
                        em.wait("tensor", "sGP", wl)
                    ce_col = one_col_f[:, :] if ei >= E else None
                    for m4 in range(4):
                        sl2 = m4 % 2
                        pg = P0.bitcast(F32)[:, sl2 * 512:sl2 * 512 + 512]
                        pu = P1.bitcast(F32)[:, sl2 * 512:sl2 * 512 + 512]
                        if silu_mark[sl2]:
                            em.wait("tensor", "sAC", silu_mark[sl2])
                        if hmul_mark[sl2]:
                            em.wait("tensor", "sVE", hmul_mark[sl2])
                        for cc in range(NCC):
                            em.op("tensor", lambda e, pg=pg, ws=ws, cc=cc, m4=m4, sg=sg: e.matmul(
                                pg, wgb[ws].rearrange("p (c m) -> p c m", c=NCC)[:, cc, ts_(m4)],
                                xr[:, cc, sg * 512:(sg + 1) * 512],
                                start=(cc == 0), stop=(cc == NCC - 1)), "sPE", 1 if cc == NCC - 1 else None)
                        gmm = em.v("sPE")
                        em.wait("scalar", "sPE", gmm)
                        if hmul_mark[sl2]:
                            em.wait("scalar", "sVE", hmul_mark[sl2])
                        em.op("scalar", lambda e, pg=pg, sl2=sl2: e.activation(
                            s_sb[sl2][:, :], pg, AF.Silu), "sAC", 1)
                        silu_mark[sl2] = em.v("sAC")
                        for cc in range(NCC):
                            em.op("tensor", lambda e, pu=pu, ws=ws, cc=cc, m4=m4, sg=sg: e.matmul(
                                pu, wub[ws].rearrange("p (c m) -> p c m", c=NCC)[:, cc, ts_(m4)],
                                xr[:, cc, sg * 512:(sg + 1) * 512],
                                start=(cc == 0), stop=(cc == NCC - 1)), "sPE", 1 if cc == NCC - 1 else None)
                        umm = em.v("sPE")
                        em.wait("vector", "sPE", umm)
                        em.wait("vector", "sAC", silu_mark[sl2])
                        em.op("vector", lambda e, pu=pu, m4=m4, sl2=sl2: e.tensor_tensor(
                            h_sb[:, ts_(m4, 512)], s_sb[sl2][:, :], pu, op=ALU.mult), "sVE", 1)
                        hmul_mark[sl2] = em.v("sVE")
                    h_done = em.v("sVE")
                    em.wait("tensor", "sVE", h_done)
                    for ti in range(4):
                        gtile = sg * 4 + ti
                        for ch in range(2):
                            dslot = (ti * 2 + ch) % 4
                            pd = (P2 if dslot < 2 else P3).bitcast(F32)[:, (dslot % 2) * 512:(dslot % 2) * 512 + 512]
                            if drot_mark[dslot]:
                                em.wait("tensor", "sVE", drot_mark[dslot])
                            for m4 in range(4):
                                em.op("tensor", lambda e, pd=pd, m4=m4, ti=ti, ws=ws, ch=ch: e.matmul(
                                    pd, h_sb[:, m4 * 512 + ti * 128: m4 * 512 + (ti + 1) * 128],
                                    wdb[ws].rearrange("p (m c) -> p m c", m=4)[:, m4, ch * 512:(ch + 1) * 512],
                                    start=(m4 == 0), stop=(m4 == 3)), "sPE", 1 if m4 == 3 else None)
                            dmm = em.v("sPE")
                            em.wait("vector", "sPE", dmm)
                            if first_stt[0]:
                                em.wait("vector", "sSY", y_ld)
                                first_stt[0] = False
                            sc_ap = ce_col if ce_col is not None else comb[:, gtile * E + ei: gtile * E + ei + 1]
                            em.op("vector", lambda e, pd=pd, gtile=gtile, ch=ch, sc_ap=sc_ap: e.scalar_tensor_tensor(
                                y_ap[:, gtile, ch * 512:(ch + 1) * 512], pd, sc_ap,
                                y_ap[:, gtile, ch * 512:(ch + 1) * 512],
                                op0=ALU.mult, op1=ALU.add), "sVE", 1)
                            drot_mark[dslot] = em.v("sVE")
                    if sg == NSG - 1:
                        wslot_free[ws] = em.v("sPE")

                em.wait("tensor", "sAC", exp_mark[NSG - 1])
                for sg in range(NSG):
                    chains(sg)
                    sg_body(E, 0, sg, loads_mark[E])
                    if sg >= 1:
                        sg_body(E + 1, 1, sg - 1, loads_mark[E + 1])
                sg_body(E + 1, 1, NSG - 1, loads_mark[E + 1])
                for ei in range(E):
                    ws = ei % 2
                    loads_mark[ei] = emit_load(ei, ws)
                    for sg in range(NSG):
                        sg_body(ei, ws, sg, loads_mark[ei])

                y_done = em.v("sVE")
                em.wait("gpsimd", "sVE", y_done)
                em.op("gpsimd", lambda e, row0=row0: e.dma_start(
                    dst[row0:row0 + TH, :].rearrange("(t p) c -> p t c", p=128), y_ap), "sGP", 16)
                wb_mark = em.v("sGP")
                prev_guards = {"sPE": marks["tr_last"], "sVE": marks["xl_last"]}
            return wb_mark

        def prep_attn_b(pe_guard):
            em.wait("gpsimd", "sPE", pe_guard)
            em.wait("tensor", "sVE")
            em.wait("tensor", "sAC")
            stage = [A_view(4 * T + 13312, 1024), A_view(4 * T + 14336, 1024)]
            tr_ps2 = [P2, P3]
            spill_mark = 0
            ev_done = {}
            idx = 0
            for (Wt, dstd) in [(bWq, wq_d), (bWk, wk_d), (bWv, wv_d), (bWo, wp_d)]:
                if spill_mark:
                    em.wait("vector", "sSY", spill_mark)
                for rr in range(NCC):
                    sl3 = idx % 2
                    if idx >= 2:
                        em.wait("gpsimd", "sPE", ev_done[(idx - 2, "tr")])
                    em.op("gpsimd", lambda e, Wt=Wt, rr=rr, sl3=sl3: e.dma_start(
                        stage[sl3][:, :], Wt[ts_(rr), :]), "sGP", 16)
                    gpm = em.v("sGP")
                    em.wait("tensor", "sGP", gpm)
                    if idx >= 2:
                        em.wait("tensor", "sVE", ev_done[(idx - 2, "ev")])
                    for cc in range(NCC):
                        em.op("tensor", lambda e, cc=cc, sl3=sl3: e.matmul(
                            tr_ps2[sl3].bitcast(BF16)[:, cc * 128:(cc + 1) * 128],
                            stage[sl3][:, ts_(cc)], ident[:, :],
                            start=(cc == 0), stop=(cc == NCC - 1), is_transpose=True),
                            "sPE", 1 if cc == NCC - 1 else None)
                    ev_done[(idx, "tr")] = em.v("sPE")
                    em.wait("vector", "sPE", ev_done[(idx, "tr")])
                    dst2 = W2[:, :].rearrange("p (c n) -> p c n", c=NCC)[:, :, rr * 128:(rr + 1) * 128]
                    em.op("vector", lambda e, dst2=dst2, sl3=sl3: e.tensor_copy(
                        dst2, tr_ps2[sl3].bitcast(BF16)[:, 0:1024].rearrange("p (c n) -> p c n", c=NCC)),
                        "sVE", 1)
                    ev_done[(idx, "ev")] = em.v("sVE")
                    idx += 1
                em.wait("sync", "sVE", em.v("sVE"))
                em.op("sync", lambda e, dstd=dstd: e.dma_start(
                    dstd.rearrange("(cc p) n -> p cc n", p=128),
                    W2[:, :].rearrange("p (c n) -> p c n", c=NCC)), "sSY", 16)
                spill_mark = em.v("sSY")

        emit_attn("a", x_in, y_out if n_sub == 1 else res_a)
        pe_a = em.v("sPE")
        wb1 = 0
        if n_sub >= 2:
            wb1 = emit_ffn(res_a, y_out if n_sub == 2 else res_b)
        if n_sub >= 3:
            prep_attn_b(pe_a)
            emit_attn("b", res_b, y_out if n_sub == 3 else res_a, gp=wb1)
        if n_sub >= 4:
            emit_ffn(res_a, y_out)

        em.wait("sync", "sSY")
        em.wait("sync", "sGP")

        @block.sync
        def _(e):
            for f in em.prog["sync"]:
                f(e)

        @block.gpsimd
        def _(e):
            for f in em.prog["gpsimd"]:
                f(e)

        @block.tensor
        def _(e):
            for f in em.prog["tensor"]:
                f(e)

        @block.scalar
        def _(e):
            for f in em.prog["scalar"]:
                f(e)

        @block.vector
        def _(e):
            for f in em.prog["vector"]:
                f(e)

    return nc


def host_constants():
    ident = np.eye(128, dtype=ml_dtypes.bfloat16)
    ident_f = np.eye(128, dtype=np.float32)
    ones_bf = np.ones((128, 128), dtype=ml_dtypes.bfloat16)
    ones_f32 = np.ones((1, 128), dtype=np.float32)
    masks = np.zeros((4, 128, 512), dtype=ml_dtypes.bfloat16)
    for d in range(4):
        ki = np.arange(128)[:, None]
        qi = np.arange(512)[None, :]
        masks[d] = np.where(128 * d + ki <= qi, 0.0, -1e30).astype(ml_dtypes.bfloat16)
    return {"c_ident": ident, "c_ident_f": ident_f, "c_ones_bf": ones_bf,
            "c_ones_f32": ones_f32, "c_masks": masks}


def shard_inputs(inputs, n_cores=8):
    w = {
        "aWq": inputs["L0_attn_a_Wq"], "aWk": inputs["L0_attn_a_Wk"], "aWv": inputs["L0_attn_a_Wv"],
        "aAq": inputs["L0_attn_a_Aq"], "aAk": inputs["L0_attn_a_Ak"], "aAv": inputs["L0_attn_a_Av"],
        "aBq": inputs["L0_attn_a_Bq"], "aBk": inputs["L0_attn_a_Bk"], "aBv": inputs["L0_attn_a_Bv"],
        "aWp": inputs["L0_attn_a_Wp"], "abp": inputs["L0_attn_a_bp"],
        "aAp": inputs["L0_attn_a_Ap"], "aBp": inputs["L0_attn_a_Bp"],
        "bWq": inputs["L0_attn_b_Wq"], "bWk": inputs["L0_attn_b_Wk"],
        "bWv": inputs["L0_attn_b_Wv"], "bWo": inputs["L0_attn_b_Wo"],
        "bbq": inputs["L0_attn_b_bq"], "bbk": inputs["L0_attn_b_bk"],
        "bbv": inputs["L0_attn_b_bv"], "bbo": inputs["L0_attn_b_bo"],
        "bthq": inputs["L0_attn_b_th_q"], "bthk": inputs["L0_attn_b_th_k"],
        "gk": inputs["L0_ffn_gate_kernel"], "gb": inputs["L0_ffn_gate_bias"],
        "wg": inputs["L0_ffn_w_gate"], "wu": inputs["L0_ffn_w_up"], "wd": inputs["L0_ffn_w_down"],
        "swg": inputs["L0_ffn_sw_gate"], "swu": inputs["L0_ffn_sw_up"], "swd": inputs["L0_ffn_sw_down"],
    }
    w = {k: np.ascontiguousarray(np.asarray(v, dtype=np.float32)) for k, v in w.items()}
    consts = host_constants()
    x = np.ascontiguousarray(np.asarray(inputs["L0_attn_a_x"], dtype=np.float32))
    maps = []
    for c in range(n_cores):
        m = {"x": np.ascontiguousarray(x[c])}
        m.update(w)
        m.update(consts)
        maps.append(m)
    return maps


def kernel(**inputs):
    n_cores = 8
    maps = shard_inputs(inputs, n_cores)
    nc = build_core(4096)
    res = run_bass_kernel_spmd(nc, maps, core_ids=list(range(n_cores)))
    return np.stack([np.asarray(r["y"], dtype=np.float32) for r in res.results], axis=0)
